# version 06
# baseline (speedup 1.0000x reference)
import numpy as np
import ml_dtypes
import concourse.bass as bass
import concourse.mybir as mybir
from concourse.bass_utils import run_bass_kernel_spmd

F32 = mybir.dt.float32
BF16 = mybir.dt.bfloat16
AF = mybir.ActivationFunctionType
ALU = mybir.AluOpType
AX = mybir.AxisListType

B, NTOK, D, NH, DH, E = 4, 4096, 1024, 16, 64, 15
T = 2048
NPAIR = 8
SC_QK = float(0.125 ** 0.5)
NEGB = 1.0e5
EPS = 1e-5

ENGS = ("tensor", "scalar", "vector", "gpsimd", "sync")


class Prog:

    def __init__(self):
        self.q = {e: [] for e in ENGS}
        self.sems = {}
        self.cnt = {}
        self.buf = {}
        self.strict = False

    def add_sem(self, name, handle):
        self.sems[name] = handle
        self.cnt[name] = 0

    def on(self, eng, fn, inc=None, n=1):
        if eng == "vector" and self.strict:
            self.wait("vector", "ve", self.cnt["ve"])
            if inc is None:
                inc = "ve"
        if inc is None:
            self.q[eng].append(fn)
            return None
        self.cnt[inc] += n
        s = self.sems[inc]
        self.q[eng].append(lambda e, fn=fn, s=s, n=n: fn(e).then_inc(s, n))
        return self.cnt[inc]

    def dma(self, eng, out, in_, sem="dm", **kw):
        self.cnt[sem] += 16
        s = self.sems[sem]
        self.q[eng].append(
            lambda e, out=out, in_=in_, s=s, kw=kw:
                e.dma_start(out=out, in_=in_, **kw).then_inc(s, 16))
        return self.cnt[sem]

    def wait(self, eng, sem, val):
        if val is None or val <= 0:
            return
        s = self.sems[sem]
        self.q[eng].append(lambda e, s=s, v=val: e.wait_ge(s, v))

    def val(self, sem):
        return self.cnt[sem]

    def reader(self, key, sem, count):
        self.buf[key] = (sem, count)

    def guard(self, eng, key):
        if key in self.buf:
            s, c = self.buf[key]
            self.wait(eng, s, c)
            del self.buf[key]


def build():
    nc = bass.Bass("TRN2", target_bir_lowering=False,
                   detect_race_conditions=False)

    io = {}
    for nm, shp in (("xt", [D, NTOK]), ("wq", [D, D]), ("wk", [D, D]),
                    ("wv", [D, D]), ("wo", [D, D]), ("bo", [128, 8]),
                    ("rw", [D, E]), ("rb", [E, 1]), ("sw1", [D, D]),
                    ("sb1", [128, 8]), ("sw2", [D, D]), ("sb2", [128, 8]),
                    ("ew1", [E, D, D]), ("eb1", [128, E * 8]),
                    ("ew2", [E, D, D]), ("eb2", [E, D]),
                    ("ident_in", [128, 128]), ("ind2_in", [2, 256]),
                    ("ind15_in", [E, E * 128])):
        io[nm] = nc.dram_tensor(nm, shp, F32, kind="ExternalInput").ap()
    io["yt"] = nc.dram_tensor("yt", [D, T], F32, kind="ExternalOutput").ap()

    base = (nc.sbuf_base + 31) // 32 * 32

    def A(nm, shp, dt, offset):
        return nc.alloc_sbuf_tensor_at(nm, shp, dt, offset=base + offset)

    t = dict(io)
    t["acc"] = A("acc", [128, 8, T], F32, offset=0)
    t["inner"] = A("inner", [128, 8, T], BF16, offset=65536)
    t["x1bf"] = A("x1bf", [128, 8, T], BF16, offset=65536)
    t["S0"] = A("S0", [128, 8, D], BF16, offset=98304)
    t["S1"] = A("S1", [128, 8, D], BF16, offset=114688)

    PA = 131072
    t["qkT"] = A("qkT", [128, 2, NTOK], BF16, offset=PA)
    t["v3d"] = A("v3d", [128, 32, 130], BF16, offset=PA + 16384)
    t["scrbf"] = A("scrbf", [128, 2048], BF16, offset=PA + 24704)
    t["bandsb"] = A("bandsb", [64, 1024], BF16, offset=PA + 26752)
    t["expsc"] = A("expsc", [128, 2, 1024], BF16, offset=PA + 28800)
    t["innerraw"] = A("innerraw", [64, T], BF16, offset=PA + 32896)
    t["fbf"] = A("fbf", [65, T], BF16, offset=PA + 36992)
    t["xstr"] = A("xstr", [128, 2, 8, 256], BF16, offset=PA + 41088)
    t["wqp"] = A("wqp", [128, 2, 8, 128], BF16, offset=PA + 49280)
    t["wkp"] = A("wkp", [128, 2, 8, 128], BF16, offset=PA + 53376)
    t["wvp"] = A("wvp", [128, 2, 8, 128], BF16, offset=PA + 57472)
    SM = PA + 61568
    for nm, shp, dt, off in (
            ("Sall", [128, 2, 32], F32, 0), ("dgdot", [128, 2, 32], F32, 256),
            ("dgexp", [128, 2, 32], F32, 512), ("keep", [128, 2, 32], F32, 768),
            ("biask", [128, 2, 32], F32, 1024), ("S2p", [128, 4], F32, 1280),
            ("sums2", [128, 4], F32, 1312), ("stbc", [128, 4], F32, 1344),
            ("negM", [128, 2], F32, 1376), ("nm15", [128, 2], F32, 1408),
            ("nmq", [2, 8], F32, 1440), ("nmk", [2, 8], F32, 1472),
            ("nqx", [2, 2], F32, 1504), ("Mv", [2, 1], F32, 1536),
            ("Mvb", [2, 1], BF16, 1568), ("statf", [1, 4], F32, 1600),
            ("tscr", [1, 4], F32, 1632), ("kqsb", [16, 128], F32, 1664),
            ("dxt", [128, 128], F32, 2176), ("ind2b", [2, 256], BF16, 2688)):
        t[nm] = A(nm, shp, dt, offset=SM + off)

    t["S2s"] = A("S2s", [128, 8, D], BF16, offset=PA)
    t["S3s"] = A("S3s", [128, 8, D], BF16, offset=PA + 16384)
    PC = PA + 32768
    for nm, shp, dt, off in (
            ("ident_c", [128, 128], F32, 0), ("ones1f_c", [1, 128], F32, 512),
            ("onescb", [128, 1], BF16, 1024), ("bos", [128, 8], F32, 1056),
            ("sb1s", [128, 8], F32, 1088), ("sb2s", [128, 8], F32, 1120),
            ("lnsc", [1, 512], F32, 2048),
            ("ybfpre", [128, 8, 512], BF16, 4096),
            ("hsc", [128, 8, 512], BF16, 12288), ("cwT", [E, T], F32, 20480),
            ("cwTb", [E, T], BF16, 28672), ("cwbcb", [128, 512], BF16, 32768),
            ("hpl", [128, 512], BF16, 33792), ("ltok", [128, 4, E], F32, 34816),
            ("ltk2", [128, 4, E], F32, 35072), ("ltk3", [128, 4, E], F32, 35328),
            ("ltm", [128, 4, 6], F32, 35584), ("eb2s", [E, D], F32, 35712),
            ("lnsm", [1, 1024], F32, 39808), ("ind15b", [E, E * 128], BF16, 43904),
            ("rwb", [128, 8, E], BF16, 47744), ("rbs", [E, 1], F32, 47984 + 16),
            ("eb1s", [128, E * 8], F32, 48032)):
        t[nm] = A(nm, shp, dt, offset=PC + off)

    PK = SM + 3200
    t["ident"] = A("ident", [128, 128], F32, offset=PK)
    t["ind65"] = A("ind65", [65, 64], BF16, offset=PK + 512)
    t["ones1f"] = A("ones1f", [1, 128], F32, offset=PK + 640)
    t["ones128f"] = A("ones128f", [128, 1], F32, offset=PK + 1152)
    t["ones2b"] = A("ones2b", [128, 2], BF16, offset=PK + 1184)

    t["P0"] = nc.alloc_psum_tensor("P0", [128, 2048], F32)
    t["P1"] = nc.alloc_psum_tensor("P1", [128, 2048], F32)

    p = Prog()
    with (
        nc.semaphore("dm") as _dm,
        nc.semaphore("sd") as _sd,
        nc.semaphore("mm") as _mm,
        nc.semaphore("sc") as _sc,
        nc.semaphore("ve") as _ve,
        nc.Block() as block,
    ):
        for nm, h in (("dm", _dm), ("sd", _sd), ("mm", _mm), ("sc", _sc), ("ve", _ve)):
            p.add_sem(nm, h)

        emit(nc, p, t)

        block.sync(lambda e: [f(e) for f in p.q["sync"]])
        block.gpsimd(lambda e: [f(e) for f in p.q["gpsimd"]])
        block.tensor(lambda e: [f(e) for f in p.q["tensor"]])
        block.scalar(lambda e: [f(e) for f in p.q["scalar"]])
        block.vector(lambda e: [f(e) for f in p.q["vector"]])
    return nc



def emit(nc, p, t):
    P0, P1 = t["P0"], t["P1"]
    acc, inner = t["acc"], t["inner"]

    p.dma("sync", t["ident"][:, :], t["ident_in"], "sd")
    for cc in range(8):
        p.dma("sync", acc[:, cc, :], t["xt"][128 * cc:128 * cc + 128, 0:T], "sd")
    ld_xres = p.val("sd")

    p.dma("gpsimd", t["ind2b"][:, :], t["ind2_in"], "dm")
    ld_ind2 = p.val("dm")

    v = t
    p.on("vector", lambda e: e.memset(v["ones1f"][:, :], 1.0))
    p.on("vector", lambda e: e.memset(v["ones128f"][:, :], 1.0))
    p.on("vector", lambda e: e.memset(v["onescb"][:, :], 1.0))
    p.on("vector", lambda e: e.memset(v["ones2b"][0:64, 0:1], 1.0))
    p.on("vector", lambda e: e.memset(v["ones2b"][64:128, 0:1], 0.0))
    p.on("vector", lambda e: e.memset(v["ones2b"][0:64, 1:2], 0.0))
    p.on("vector", lambda e: e.memset(v["ones2b"][64:128, 1:2], 1.0))
    p.on("vector", lambda e: e.memset(v["ind65"][0:64, :], 0.0))
    p.on("vector", lambda e: e.memset(v["ind65"][64:65, :], 1.0))
    p.on("vector", lambda e: e.memset(v["fbf"][0:64, :], 0.0))
    p.on("vector", lambda e: e.memset(v["v3d"][:, :, 64:65], 1.0))
    ve_const = p.on("vector", lambda e: e.memset(v["v3d"][:, :, 129:130], 1.0), inc="ve")

    st = dict(ld_ind2=ld_ind2, ve_const=ve_const)

    p.dma("sync", t["eb1s"][:, :], t["eb1"], "sd")
    p.dma("sync", t["eb2s"][:, :], t["eb2"], "sd")
    p.dma("sync", t["rbs"][:, :], t["rb"], "sd")
    ld_late = p.val("sd")

    ld_wo = ld_rw = ld_ind15 = None
    for pair in range(NPAIR):
        emit_pair(nc, p, t, pair, st)
        if pair == 0:
            for cc in range(8):
                p.dma("gpsimd", t["S0"][:, cc, :], t["wo"][128 * cc:128 * cc + 128, :], "dm")
            ld_wo = p.val("dm")
            p.dma("gpsimd", t["rwb"][:, :, :], t["rw"].rearrange("(c p) e -> p c e", p=128), "dm")
            ld_rw = p.val("dm")
            p.dma("gpsimd", t["ind15b"][:, :], t["ind15_in"], "dm")
            ld_ind15 = p.val("dm")

    attn_mm = p.val("mm")
    attn_sc = p.val("sc")
    attn_ve = p.val("ve")
    attn_dm = p.val("dm")

    p.wait("sync", "ve", attn_ve)
    p.wait("sync", "sc", attn_sc)
    p.wait("sync", "mm", attn_mm)
    p.wait("sync", "dm", attn_dm)
    p.dma("sync", t["ident_c"][:, :], t["ident_in"], "sd")
    p.dma("sync", t["bos"][:, :], t["bo"], "sd")
    p.dma("sync", t["sb1s"][:, :], t["sb1"], "sd")
    p.dma("sync", t["sb2s"][:, :], t["sb2"], "sd")
    ld_consts = p.val("sd")
    p.wait("vector", "dm", attn_dm)
    p.on("vector", lambda e: e.memset(t["ones1f_c"][:, :], 1.0))
    p.on("vector", lambda e: e.memset(t["onescb"][:, :], 1.0), inc="ve")

    p.wait("gpsimd", "ve", attn_ve)
    p.wait("gpsimd", "sc", attn_sc)
    p.wait("gpsimd", "mm", attn_mm)
    for cc in range(8):
        p.dma("gpsimd", t["S2s"][:, cc, :], t["sw1"][128 * cc:128 * cc + 128, :], "dm")
    for cc in range(8):
        p.dma("gpsimd", t["S3s"][:, cc, :], t["sw2"][128 * cc:128 * cc + 128, :], "dm")
    ld_sw = p.val("dm")

    emit_outproj(nc, p, t, ld_xres, ld_wo, ld_consts, attn_ve)
    emit_ln(nc, p, t, out_bf=True)
    b_mm = p.val("mm")

    p.wait("gpsimd", "mm", b_mm)
    p.dma("gpsimd", t["S0"][:, :, :], t["ew1"][0].rearrange("(c p) h -> p c h", p=128), "dm")
    p.dma("gpsimd", t["S1"][:, :, :], t["ew2"][0].rearrange("(c p) h -> p c h", p=128), "dm")
    ld_e0 = p.val("dm")

    emit_ffn(nc, p, t, ld_sw, ld_e0, ld_late, ld_rw, ld_ind15)

    emit_ln(nc, p, t, out_bf=False)
    d_ve = p.val("ve")
    p.wait("sync", "ve", d_ve)
    p.dma("sync", t["yt"].rearrange("(c p) t -> p c t", p=128), acc[:, :, :], "sd")
    p.wait("sync", "sd", p.val("sd"))


def emit_pair(nc, p, t, pair, st):
    P0, P1 = t["P0"], t["P1"]
    qkT, v3d, scrbf, expsc = t["qkT"], t["v3d"], t["scrbf"], t["expsc"]
    xstr, wqp, wkp, wvp = t["xstr"], t["wqp"], t["wkp"], t["wvp"]
    bandsb, innerraw, fbf, inner = t["bandsb"], t["innerraw"], t["fbf"], t["inner"]
    ident, ind65, ind2b = t["ident"], t["ind65"], t["ind2b"]
    ones1f, ones128f, ones2b = t["ones1f"], t["ones128f"], t["ones2b"]
    Sall, dgdot, dgexp, keepv, biask = t["Sall"], t["dgdot"], t["dgexp"], t["keep"], t["biask"]
    S2p, sums2, stbc, negM, nm15 = t["S2p"], t["sums2"], t["stbc"], t["negM"], t["nm15"]
    nmq, nmk, nqx, Mv, Mvb, statf, tscr = t["nmq"], t["nmk"], t["nqx"], t["Mv"], t["Mvb"], t["statf"], t["tscr"]
    kqsb, dxt = t["kqsb"], t["dxt"]
    xt, wq, wk, wv = t["xt"], t["wq"], t["wk"], t["wv"]

    col0 = 128 * pair
    wb = pair % 2

    p.guard("gpsimd", ("wqp", wb))
    for cc in range(8):
        p.dma("gpsimd", wqp[:, wb, cc, :], wq[128 * cc:128 * cc + 128, col0:col0 + 128], "dm")
        p.dma("gpsimd", wkp[:, wb, cc, :], wk[128 * cc:128 * cc + 128, col0:col0 + 128], "dm")
        p.dma("gpsimd", wvp[:, wb, cc, :], wv[128 * cc:128 * cc + 128, col0:col0 + 128], "dm")
    ld_w = p.val("dm")

    pq = [P0[:, 0:256], P0[:, 512:768]]
    pk = [P0[:, 1024:1280], P0[:, 1536:1792]]
    pv = [P1[:, 0:128], P1[:, 512:640]]
    pnq, pnk = P1[0:2, 1024:1536], P1[0:2, 1536:2048]

    for tc in range(16):
        xb = tc % 2
        p.guard("gpsimd", ("xstr", xb))
        for cc in range(8):
            p.dma("gpsimd", xstr[:, xb, cc, :],
                  xt[128 * cc:128 * cc + 128, 256 * tc:256 * tc + 256], "dm")
        xld = p.val("dm")

        pb = tc % 2
        p.wait("tensor", "dm", max(ld_w, xld))
        p.guard("tensor", ("pq", pb))
        for cc in range(8):
            p.on("tensor", lambda e, cc=cc, pb=pb, wb=wb, xb=xb: e.matmul(
                pq[pb], wqp[:, wb, cc, :], xstr[:, xb, cc, :],
                start=(cc == 0), stop=(cc == 7)), inc="mm" if cc == 7 else None)
        mmq = p.val("mm")
        p.guard("tensor", ("pk", pb))
        for cc in range(8):
            p.on("tensor", lambda e, cc=cc, pb=pb, wb=wb, xb=xb: e.matmul(
                pk[pb], wkp[:, wb, cc, :], xstr[:, xb, cc, :],
                start=(cc == 0), stop=(cc == 7)), inc="mm" if cc == 7 else None)
        mmk = p.val("mm")
        for tt4 in range(2):
            tt = 2 * tc + tt4
            vb = tt % 2
            p.guard("tensor", ("pv", vb))
            if tt == 0:
                p.wait("tensor", "ve", st["ve_const"])
            for cc in range(8):
                p.on("tensor", lambda e, cc=cc, vb=vb, wb=wb, xb=xb, tt4=tt4: e.matmul(
                    pv[vb], xstr[:, xb, cc, 128 * tt4:128 * tt4 + 128], wvp[:, wb, cc, :],
                    start=(cc == 0), stop=(cc == 7)), inc="mm" if cc == 7 else None)
            mmv = p.val("mm")
            p.wait("vector", "mm", mmv)
            p.on("vector", lambda e, tt=tt, vb=vb: e.tensor_copy(v3d[:, tt, 0:64], pv[vb][:, 0:64]))
            vv = p.on("vector", lambda e, tt=tt, vb=vb: e.tensor_copy(v3d[:, tt, 65:129], pv[vb][:, 64:128]), inc="ve")
            p.reader(("pv", vb), "ve", vv)
        p.reader(("xstr", xb), "mm", p.val("mm"))

        p.wait("scalar", "mm", mmq)
        sq = p.on("scalar", lambda e, tc=tc, pb=pb: e.activation(
            qkT[:, 0, 256 * tc:256 * tc + 256], pq[pb], AF.Copy, scale=SC_QK), inc="sc")
        p.reader(("pq", pb), "sc", sq)
        p.wait("scalar", "mm", mmk)
        sk = p.on("scalar", lambda e, tc=tc, pb=pb: e.activation(
            qkT[:, 1, 256 * tc:256 * tc + 256], pk[pb], AF.Copy, scale=SC_QK), inc="sc")
        p.reader(("pk", pb), "sc", sk)
    proj_sc = p.val("sc")
    p.reader(("wqp", wb), "mm", p.val("mm"))
    p.reader(("xstr", 0), "mm", p.val("mm"))
    p.reader(("xstr", 1), "mm", p.val("mm"))

    last_red = None
    for si, (pn, nmx) in enumerate(((pnq, nmq), (pnk, nmk))):
        for half in range(2):
            p.wait("vector", "sc", proj_sc)
            p.guard("vector", ("scrbf",))
            sqv = p.on("vector", lambda e, si=si, half=half: e.tensor_mul(
                scrbf[:, :], qkT[:, si, 2048 * half:2048 * half + 2048],
                qkT[:, si, 2048 * half:2048 * half + 2048]), inc="ve")
            for c4 in range(4):
                p.wait("tensor", "ve", sqv)
                if last_red:
                    p.wait("tensor", "ve", last_red)
                mmn = p.on("tensor", lambda e, c4=c4, pn=pn: e.matmul(
                    pn, ones2b[:, :], scrbf[:, 512 * c4:512 * c4 + 512],
                    start=True, stop=True), inc="mm")
                p.wait("vector", "mm", mmn)
                last_red = p.on("vector", lambda e, half=half, c4=c4, pn=pn, nmx=nmx: e.reduce_max(
                    nmx[:, 4 * half + c4:4 * half + c4 + 1], pn, axis=AX.X), inc="ve")
            p.reader(("scrbf",), "mm", p.val("mm"))
    p.strict = True
    p.on("vector", lambda e: e.reduce_max(nqx[:, 0:1], nmq[:, :], axis=AX.X))
    p.on("vector", lambda e: e.reduce_max(nqx[:, 1:2], nmk[:, :], axis=AX.X))
    p.on("vector", lambda e: e.tensor_add(Mv[:, :], nqx[:, 0:1], nqx[:, 1:2]))
    p.on("vector", lambda e: e.tensor_scalar_mul(Mv[:, :], Mv[:, :], 0.5))
    mvb_ve = p.on("vector", lambda e: e.tensor_copy(Mvb[:, :], Mv[:, :]), inc="ve")

    pMbc = P1[:, 1024:1026]
    p.wait("tensor", "ve", mvb_ve)
    p.wait("tensor", "dm", st["ld_ind2"])
    for h in range(2):
        p.on("tensor", lambda e, h=h: e.matmul(
            pMbc[:, h:h + 1], ind2b[:, 128 * h:128 * h + 128], Mvb[:, :],
            start=True, stop=True), inc="mm" if h == 1 else None)
    mmbc = p.val("mm")
    p.wait("vector", "mm", mmbc)
    p.on("vector", lambda e: e.tensor_scalar_mul(negM[:, :], pMbc[:, :], -1.0))
    negm_ve = p.on("vector", lambda e: e.tensor_scalar_add(nm15[:, :], negM[:, :], -NEGB), inc="ve")
    p.strict = False

    pA, pB = P0[:, 0:2048], P1[:, 0:2048]
    prev_sc = None
    prev_dg = None
    prev_Sve = None
    first = True
    for qt in range(32):
        gstar = qt // 16
        for grp in range(2):
            if first:
                p.wait("tensor", "sc", proj_sc)
                p.wait("tensor", "ve", negm_ve)
                first = False
            if prev_sc:
                p.wait("tensor", "sc", prev_sc)
            if prev_dg:
                p.wait("tensor", "ve", prev_dg)
                prev_dg = None
            for kc in range(4):
                p.on("tensor", lambda e, qt=qt, grp=grp, kc=kc: e.matmul(
                    pA[:, 512 * kc:512 * kc + 512],
                    qkT[0:64, 0, 128 * qt:128 * qt + 128],
                    qkT[0:64, 1, 2048 * grp + 512 * kc:2048 * grp + 512 * kc + 512],
                    start=True, stop=True, tile_position=(0, 0)))
                p.on("tensor", lambda e, qt=qt, grp=grp, kc=kc: e.matmul(
                    pB[:, 512 * kc:512 * kc + 512],
                    qkT[64:128, 0, 128 * qt:128 * qt + 128],
                    qkT[64:128, 1, 2048 * grp + 512 * kc:2048 * grp + 512 * kc + 512],
                    start=True, stop=True, tile_position=(64, 0)),
                    inc="mm" if kc == 3 else None)
            mmp = p.val("mm")
            p.wait("scalar", "mm", mmp)
            if prev_Sve:
                p.wait("scalar", "ve", prev_Sve)
                prev_Sve = None
            if qt == 0 and grp == 0:
                p.wait("scalar", "ve", negm_ve)
            p.guard("scalar", ("scrbf",))
            p.on("scalar", lambda e, grp=grp: e.activation(
                scrbf[:, :], pA, AF.Exp, bias=negM[:, 0:1],
                accum_out=S2p[:, 2 * grp:2 * grp + 1]))
            prev_sc = p.on("scalar", lambda e, grp=grp: e.activation(
                scrbf[:, :], pB, AF.Exp, bias=negM[:, 1:2],
                accum_out=S2p[:, 2 * grp + 1:2 * grp + 2]), inc="sc")
            if grp == gstar:
                off = (128 * qt) % 2048
                p.wait("vector", "sc", prev_sc)
                p.on("vector", lambda e, off=off: e.tensor_mul(dxt[:, :], pA[:, off:off + 128], ident[:, :]))
                p.on("vector", lambda e, qt=qt: e.reduce_sum(dgdot[:, 0, qt:qt + 1], dxt[:, :], axis=AX.X))
                p.on("vector", lambda e, off=off: e.tensor_mul(dxt[:, :], pB[:, off:off + 128], ident[:, :]))
                prev_dg = p.on("vector", lambda e, qt=qt: e.reduce_sum(dgdot[:, 1, qt:qt + 1], dxt[:, :], axis=AX.X), inc="ve")
        p.wait("vector", "sc", prev_sc)
        p.on("vector", lambda e, qt=qt: e.tensor_add(Sall[:, 0, qt:qt + 1], S2p[:, 0:1], S2p[:, 2:3]))
        prev_Sve = p.on("vector", lambda e, qt=qt: e.tensor_add(Sall[:, 1, qt:qt + 1], S2p[:, 1:2], S2p[:, 3:4]), inc="ve")
    pass1_sc = p.val("sc")

    p.wait("scalar", "ve", p.val("ve"))
    sc_dg = None
    for h in range(2):
        sc_dg = p.on("scalar", lambda e, h=h: e.activation(
            dgexp[:, h, :], dgdot[:, h, :], AF.Exp, bias=negM[:, h:h + 1]), inc="sc")
    p.wait("vector", "sc", sc_dg)
    p.strict = True
    p.on("vector", lambda e: e.reciprocal(
        dxt[:, 0:64], Sall[:, :, :].rearrange("p a b -> p (a b)")))
    p.on("vector", lambda e: e.tensor_mul(
        dgexp[:, :, :].rearrange("p a b -> p (a b)"),
        dgexp[:, :, :].rearrange("p a b -> p (a b)"), dxt[:, 0:64]))
    for h in range(2):
        p.on("vector", lambda e, h=h: e.reduce_sum(sums2[:, 2 * h:2 * h + 1], dgexp[:, h, :], axis=AX.X))
        p.on("vector", lambda e, h=h: e.tensor_mul(dxt[:, 0:32], dgexp[:, h, :], dgexp[:, h, :]))
        p.on("vector", lambda e, h=h: e.reduce_sum(sums2[:, 2 * h + 1:2 * h + 2], dxt[:, 0:32], axis=AX.X), inc="ve")
    st_ve = p.val("ve")
    pst = P0[0:1, 0:4]
    p.wait("tensor", "ve", st_ve)
    mmst = p.on("tensor", lambda e: e.matmul(pst, ones128f[:, :], sums2[:, :], start=True, stop=True), inc="mm")
    p.wait("vector", "mm", mmst)
    n = float(NTOK)
    p.on("vector", lambda e: e.tensor_copy(tscr[:, :], pst))
    for h in range(2):
        p.on("vector", lambda e, h=h: e.tensor_scalar_mul(statf[:, 2 * h:2 * h + 1], tscr[:, 2 * h:2 * h + 1], 1.0 / n))
        p.on("vector", lambda e, h=h: e.tensor_mul(statf[:, 2 * h + 1:2 * h + 2], tscr[:, 2 * h:2 * h + 1], tscr[:, 2 * h:2 * h + 1]))
        p.on("vector", lambda e, h=h: e.tensor_scalar_mul(statf[:, 2 * h + 1:2 * h + 2], statf[:, 2 * h + 1:2 * h + 2], 1.0 / n))
        p.on("vector", lambda e, h=h: e.tensor_sub(statf[:, 2 * h + 1:2 * h + 2], tscr[:, 2 * h + 1:2 * h + 2], statf[:, 2 * h + 1:2 * h + 2]))
        p.on("vector", lambda e, h=h: e.tensor_scalar_mul(statf[:, 2 * h + 1:2 * h + 2], statf[:, 2 * h + 1:2 * h + 2], 2.25 / (n - 1.0)), inc="ve")
    stf_ve = p.val("ve")
    pbc = P0[:, 512:516]
    p.wait("tensor", "ve", stf_ve)
    mmbc2 = p.on("tensor", lambda e: e.matmul(pbc, ones1f[:, :], statf[:, :], start=True, stop=True), inc="mm")
    p.wait("vector", "mm", mmbc2)
    p.on("vector", lambda e: e.tensor_copy(stbc[:, :], pbc), inc="ve")
    for h in range(2):
        p.on("vector", lambda e, h=h: e.tensor_scalar(dxt[:, 0:32], dgexp[:, h, :], stbc[:, 2 * h:2 * h + 1], None, ALU.subtract))
        p.on("vector", lambda e: e.tensor_mul(dxt[:, 0:32], dxt[:, 0:32], dxt[:, 0:32]))
        p.on("vector", lambda e, h=h: e.tensor_scalar(keepv[:, h, :], dxt[:, 0:32], stbc[:, 2 * h + 1:2 * h + 2], None, ALU.is_le))
        p.on("vector", lambda e, h=h: e.tensor_scalar(biask[:, h, :], keepv[:, h, :], NEGB, nm15[:, h:h + 1], ALU.mult, ALU.add), inc="ve")
    keep_ve = p.val("ve")
    p.strict = False

    ping = [P0[:, 0:1024], P0[:, 1024:2048]]
    pav = P1[0:65, 0:1024]
    pband = P1[0:64, 1024:2048]
    prow = P1[0:16, 1024:1152]
    keeprow = fbf[64:65, :]
    bandsrc = fbf[0:65, :]
    ev_exp = [None, None]
    prev_norm_ve = None
    for h in range(2):
        hs = 64 * h
        p.wait("tensor", "ve", keep_ve if h == 0 else prev_norm_ve)
        mkq = p.on("tensor", lambda e, h=h: e.transpose(prow, keepv[:, h, 0:16], ident[:, :]), inc="mm")
        p.wait("vector", "mm", mkq)
        p.guard("vector", ("kqsb",))
        vkq = p.on("vector", lambda e: e.tensor_copy(kqsb[:, :], prow), inc="ve")
        p.wait("gpsimd", "ve", vkq)
        p.guard("gpsimd", ("fbfrow",))
        kqd = p.dma("gpsimd", fbf[64:65, 0:2048], kqsb[:, :], "dm")
        p.reader(("kqsb",), "dm", kqd)
        kq_done = p.val("dm")

        for qh in range(2):
            q0 = 1024 * qh
            for kt in range(33):
                if kt < 32:
                    pb = kt % 2
                    if ev_exp[pb]:
                        p.wait("tensor", "sc", ev_exp[pb])
                    if h == 0 and qh == 0 and kt == 0:
                        p.wait("tensor", "ve", p.val("ve"))
                    for qc in range(2):
                        p.on("tensor", lambda e, hs=hs, kt=kt, q0=q0, qc=qc, pb=pb: e.matmul(
                            ping[pb][:, 512 * qc:512 * qc + 512],
                            qkT[hs:hs + 64, 1, 128 * kt:128 * kt + 128],
                            qkT[hs:hs + 64, 0, q0 + 512 * qc:q0 + 512 * qc + 512],
                            start=True, stop=True, tile_position=(hs, 0)),
                            inc="mm" if qc == 1 else None)
                    mm2 = p.val("mm")
                    p.wait("scalar", "mm", mm2)
                    if h == 0 and qh == 0 and kt == 0:
                        p.wait("scalar", "ve", keep_ve)
                    p.guard("scalar", ("expsc", pb))
                    ev_exp[pb] = p.on("scalar", lambda e, h=h, kt=kt, pb=pb: e.activation(
                        expsc[:, pb, :], ping[pb], AF.Exp, bias=biask[:, h, kt:kt + 1]), inc="sc")
                if kt >= 1:
                    akt = kt - 1
                    apb = akt % 2
                    p.wait("tensor", "sc", ev_exp[apb])
                    if akt == 0 and prev_norm_ve:
                        p.wait("tensor", "ve", prev_norm_ve)
                    for qc in range(2):
                        p.on("tensor", lambda e, h=h, akt=akt, qc=qc, apb=apb: e.matmul(
                            pav[:, 512 * qc:512 * qc + 512],
                            v3d[:, akt, 65 * h:65 * h + 65],
                            expsc[:, apb, 512 * qc:512 * qc + 512],
                            start=(akt == 0), stop=(akt == 31)),
                            inc="mm" if qc == 1 else None)
                    p.reader(("expsc", apb), "mm", p.val("mm"))
            mm_av = p.val("mm")
            p.wait("vector", "mm", mm_av)
            if qh == 0:
                p.wait("vector", "dm", kq_done)
            p.on("vector", lambda e: e.reciprocal(pav[64:65, :], pav[64:65, :]))
            f_ve = p.on("vector", lambda e, q0=q0: e.tensor_mul(
                keeprow[:, q0:q0 + 1024], pav[64:65, :], keeprow[:, q0:q0 + 1024]), inc="ve")
            p.wait("tensor", "ve", f_ve)
            p.on("tensor", lambda e, q0=q0: e.matmul(
                pband[:, 0:512], ind65[:, :], bandsrc[:, q0:q0 + 512],
                start=True, stop=True))
            mmband = p.on("tensor", lambda e, q0=q0: e.matmul(
                pband[:, 512:1024], ind65[:, :], bandsrc[:, q0 + 512:q0 + 1024],
                start=True, stop=True), inc="mm")
            if h == 0 and qh == 1:
                p.reader(("fbfrow",), "mm", p.val("mm"))
            p.wait("vector", "mm", mmband)
            p.on("vector", lambda e: e.tensor_copy(bandsb[:, :], pband))
            if h == 0:
                prev_norm_ve = p.on("vector", lambda e, q0=q0, pair=pair: e.tensor_mul(
                    inner[0:64, pair, q0:q0 + 1024], pav[0:64, :], bandsb[:, :]), inc="ve")
            else:
                p.guard("vector", ("innerraw", qh))
                prev_norm_ve = p.on("vector", lambda e, q0=q0: e.tensor_mul(
                    innerraw[:, q0:q0 + 1024], pav[0:64, :], bandsb[:, :]), inc="ve")
                p.wait("gpsimd", "ve", prev_norm_ve)
                dmi = p.dma("gpsimd", inner[64:128, pair, q0:q0 + 1024], innerraw[:, q0:q0 + 1024], "dm")
                p.reader(("innerraw", qh), "dm", dmi)
    p.reader(("fbfrow",), "mm", p.val("mm"))
    p.reader(("scrbf",), "mm", p.val("mm"))


def emit_outproj(nc, p, t, ld_xres, ld_wo, ld_consts, attn_ve):
    P0 = t["P0"]
    acc, inner, S0, bos = t["acc"], t["inner"], t["S0"], t["bos"]
    pop = [P0[:, 0:512], P0[:, 512:1024]]
    evb = [None, None]
    p.wait("tensor", "dm", ld_wo)
    p.wait("tensor", "ve", attn_ve)
    for tc in range(4):
        t0 = 512 * tc
        for dc in range(8):
            pb = (8 * tc + dc) % 2
            if evb[pb]:
                p.wait("tensor", "ve", evb[pb])
            for cc in range(8):
                p.on("tensor", lambda e, cc=cc, dc=dc, t0=t0, pb=pb: e.matmul(
                    pop[pb], S0[:, cc, 128 * dc:128 * dc + 128],
                    inner[:, cc, t0:t0 + 512],
                    start=(cc == 0), stop=(cc == 7)), inc="mm" if cc == 7 else None)
            mmo = p.val("mm")
            p.wait("scalar", "mm", mmo)
            if tc == 0 and dc == 0:
                p.wait("scalar", "sd", ld_consts)
            sco = p.on("scalar", lambda e, dc=dc, pb=pb: e.add(
                pop[pb], pop[pb], bos[:, dc:dc + 1]), inc="sc")
            p.wait("vector", "sc", sco)
            if tc == 0 and dc == 0:
                p.wait("vector", "sd", ld_xres)
            evb[pb] = p.on("vector", lambda e, dc=dc, t0=t0, pb=pb: e.tensor_add(
                acc[:, dc, t0:t0 + 512], acc[:, dc, t0:t0 + 512], pop[pb]), inc="ve")


def emit_ln(nc, p, t, out_bf):
    P1 = t["P1"]
    acc, x1bf = t["acc"], t["x1bf"]
    onescb, ones1f = t["onescb"], t["ones1f_c"]
    ybfpre, hsc, lnsm = t["ybfpre"], t["hsc"], t["lnsm"]

    ps, ps2 = P1[0:1, 0:512], P1[0:1, 512:1024]
    pmu, prs = P1[:, 1024:1536], P1[:, 1536:2048]

    p.wait("tensor", "sc", p.val("sc"))
    for tc in range(4):
        t0 = 512 * tc
        cast_ve = None
        for dc in range(8):
            p.on("vector", lambda e, dc=dc, t0=t0: e.tensor_copy(ybfpre[:, dc, :], acc[:, dc, t0:t0 + 512]))
            cast_ve = p.on("vector", lambda e, dc=dc: e.tensor_mul(hsc[:, dc, :], ybfpre[:, dc, :], ybfpre[:, dc, :]), inc="ve")
        p.wait("tensor", "ve", cast_ve)
        for dc in range(8):
            p.on("tensor", lambda e, dc=dc: e.matmul(
                ps, onescb[:, :], ybfpre[:, dc, :], start=(dc == 0), stop=(dc == 7)),
                inc="mm" if dc == 7 else None)
        for dc in range(8):
            p.on("tensor", lambda e, dc=dc: e.matmul(
                ps2, onescb[:, :], hsc[:, dc, :], start=(dc == 0), stop=(dc == 7)),
                inc="mm" if dc == 7 else None)
        mm_s = p.val("mm")
        lnsc = t["lnsc"]
        p.wait("vector", "mm", mm_s)
        p.on("vector", lambda e: e.tensor_scalar_mul(lnsm[:, 0:512], ps, 1.0 / 1024.0))
        p.on("vector", lambda e, lnsc=lnsc: e.tensor_scalar_mul(lnsc[:, :], ps2, 1.0 / 1024.0))
        p.on("vector", lambda e: e.tensor_mul(lnsm[:, 512:1024], lnsm[0:1, 0:512], lnsm[0:1, 0:512]))
        p.on("vector", lambda e, lnsc=lnsc: e.tensor_sub(lnsc[:, :], lnsc[:, :], lnsm[0:1, 512:1024]))
        p.on("vector", lambda e, lnsc=lnsc: e.tensor_scalar_add(lnsc[:, :], lnsc[:, :], EPS))
        rec_ve = p.on("vector", lambda e, lnsc=lnsc: e.reciprocal(lnsc[:, :], lnsc[:, :]), inc="ve")
        p.wait("scalar", "ve", rec_ve)
        sq_sc = p.on("scalar", lambda e, lnsc=lnsc: e.activation(lnsm[:, 512:1024], lnsc[:, :], AF.Sqrt), inc="sc")
        p.wait("tensor", "sc", sq_sc)
        p.on("tensor", lambda e: e.matmul(pmu, ones1f[:, :], lnsm[0:1, 0:512], start=True, stop=True))
        mmb = p.on("tensor", lambda e: e.matmul(prs, ones1f[:, :], lnsm[0:1, 512:1024], start=True, stop=True), inc="mm")
        p.wait("vector", "mm", mmb)
        for dc in range(8):
            p.on("vector", lambda e, dc=dc, t0=t0: e.tensor_sub(acc[:, dc, t0:t0 + 512], acc[:, dc, t0:t0 + 512], pmu))
            lv = p.on("vector", lambda e, dc=dc, t0=t0: e.tensor_mul(acc[:, dc, t0:t0 + 512], acc[:, dc, t0:t0 + 512], prs), inc="ve")
            if out_bf:
                lv = p.on("vector", lambda e, dc=dc, t0=t0: e.tensor_copy(x1bf[:, dc, t0:t0 + 512], acc[:, dc, t0:t0 + 512]), inc="ve")


def _b3(ap):
    return ap.broadcast_to([128, 4, E])


def emit_ffn(nc, p, t, ld_sw, ld_e0, ld_late, ld_rw, ld_ind15):
    P0, P1 = t["P0"], t["P1"]
    acc, x1bf = t["acc"], t["x1bf"]
    S0, S1, S2s, S3s = t["S0"], t["S1"], t["S2s"], t["S3s"]
    rwb, rbs, cwT, cwTb = t["rwb"], t["rbs"], t["cwT"], t["cwTb"]
    ltok, ltk2, ltk3, ltm = t["ltok"], t["ltk2"], t["ltk3"], t["ltm"]
    ident, ind15b, eb2s = t["ident_c"], t["ind15b"], t["eb2s"]
    ew1, ew2 = t["ew1"], t["ew2"]

    plg = P0[0:15, 0:512]
    ptt = P0[:, 512:527]
    pct = P0[0:15, 1024:1152]
    pye = [P1[:, 1024:1536], P1[:, 1536:2048]]

    ln1_ve = p.val("ve")
    p.wait("tensor", "dm", ld_rw)
    p.wait("tensor", "ve", ln1_ve)
    p.strict = True
    for tc in range(4):
        t0 = 512 * tc
        p.guard("tensor", ("plg",))
        for cc in range(8):
            p.on("tensor", lambda e, cc=cc, t0=t0: e.matmul(
                plg, rwb[:, cc, :], x1bf[:, cc, t0:t0 + 512],
                start=(cc == 0), stop=(cc == 7)), inc="mm" if cc == 7 else None)
        mml = p.val("mm")
        p.wait("scalar", "mm", mml)
        if tc == 0:
            p.wait("scalar", "sd", ld_late)
        scl = p.on("scalar", lambda e, t0=t0: e.add(
            cwT[0:15, t0:t0 + 512], plg, rbs[:, 0:1]), inc="sc")
        p.reader(("plg",), "sc", scl)
        for j in range(4):
            p.wait("tensor", "sc", scl)
            p.guard("tensor", ("ptt",))
            mmt = p.on("tensor", lambda e, t0=t0, j=j: e.transpose(
                ptt, cwT[0:15, t0 + 128 * j:t0 + 128 * j + 128], ident[0:15, 0:15]), inc="mm")
            p.wait("vector", "mm", mmt)
            vtt = p.on("vector", lambda e, j=j: e.tensor_copy(ltok[:, j, :], ptt), inc="ve")
            p.reader(("ptt",), "ve", vtt)

        p.on("vector", lambda e: e.reduce_max(ltm[:, :, 0:1], ltok[:, :, :], axis=AX.X))
        sub_ve = p.on("vector", lambda e: e.tensor_tensor(ltk2[:, :, :], ltok[:, :, :], _b3(ltm[:, :, 0:1]), ALU.subtract), inc="ve")
        p.wait("scalar", "ve", sub_ve)
        exp_sc = p.on("scalar", lambda e: e.activation(ltk2[:, :, :], ltk2[:, :, :], AF.Exp), inc="sc")
        p.wait("vector", "sc", exp_sc)
        p.on("vector", lambda e: e.reduce_sum(ltm[:, :, 1:2], ltk2[:, :, :], axis=AX.X))
        p.on("vector", lambda e: e.reciprocal(ltm[:, :, 2:3], ltm[:, :, 1:2]))
        p.on("vector", lambda e: e.tensor_tensor(ltok[:, :, :], ltk2[:, :, :], _b3(ltm[:, :, 2:3]), ALU.mult))
        p.on("vector", lambda e: e.reduce_max(ltm[:, :, 3:4], ltok[:, :, :], axis=AX.X))
        p.on("vector", lambda e: e.tensor_tensor(ltk3[:, :, :], ltok[:, :, :], _b3(ltm[:, :, 3:4]), ALU.is_ge))
        p.on("vector", lambda e: e.scalar_tensor_tensor(ltk2[:, :, :], ltk3[:, :, :], -1e9, ltok[:, :, :], ALU.mult, ALU.add))
        p.on("vector", lambda e: e.reduce_max(ltm[:, :, 4:5], ltk2[:, :, :], axis=AX.X))
        p.on("vector", lambda e: e.tensor_tensor(ltk3[:, :, :], ltk2[:, :, :], _b3(ltm[:, :, 4:5]), ALU.is_ge))
        p.on("vector", lambda e: e.scalar_tensor_tensor(ltk2[:, :, :], ltk3[:, :, :], -1e9, ltk2[:, :, :], ALU.mult, ALU.add))
        p.on("vector", lambda e: e.reduce_max(ltm[:, :, 5:6], ltk2[:, :, :], axis=AX.X))
        p.on("vector", lambda e: e.tensor_tensor(ltk3[:, :, :], ltok[:, :, :], _b3(ltm[:, :, 5:6]), ALU.is_ge))
        cw_ve = p.on("vector", lambda e: e.tensor_mul(ltk3[:, :, :], ltk3[:, :, :], ltok[:, :, :]), inc="ve")
        for j in range(4):
            p.wait("tensor", "ve", cw_ve)
            p.guard("tensor", ("pct",))
            mmc = p.on("tensor", lambda e, j=j: e.transpose(pct, ltk3[:, j, :], ident[:, :]), inc="mm")
            p.wait("scalar", "mm", mmc)
            scc = p.on("scalar", lambda e, t0=t0, j=j: e.copy(cwT[0:15, t0 + 128 * j:t0 + 128 * j + 128], pct), inc="sc")
            p.reader(("pct",), "sc", scc)
    cw_sc = p.val("sc")
    p.wait("vector", "sc", cw_sc)
    cwb_ve = p.on("vector", lambda e: e.tensor_copy(cwTb[:, :], cwT[:, :]), inc="ve")
    p.strict = False

    p.wait("tensor", "ve", cwb_ve)
    for dc in range(8):
        for tc in range(4):
            pb = (4 * tc + dc) % 2
            p.guard("tensor", ("pye", pb))
            mme = p.on("tensor", lambda e, dc=dc, tc=tc, pb=pb: e.matmul(
                pye[pb], eb2s[:, 128 * dc:128 * dc + 128], cwT[0:15, 512 * tc:512 * tc + 512],
                start=True, stop=True), inc="mm")
            p.wait("vector", "mm", mme)
            va = p.on("vector", lambda e, dc=dc, tc=tc, pb=pb: e.tensor_add(
                acc[:, dc, 512 * tc:512 * tc + 512], acc[:, dc, 512 * tc:512 * tc + 512], pye[pb]), inc="ve")
            p.reader(("pye", pb), "ve", va)

    emit_mlp(nc, p, t, w1=S2s, w2=S3s, e_idx=None, wait_dm=ld_sw, wait_ind=None)
    slots = [(S0, S1), (S2s, S3s)]
    slot_free_mm = [None, p.val("mm")]
    ld_prev = ld_e0
    for ei in range(E):
        sl = slots[ei % 2]
        if ei + 1 < E:
            osl = slots[(ei + 1) % 2]
            p.wait("gpsimd", "mm", slot_free_mm[(ei + 1) % 2] or 0)
            p.dma("gpsimd", osl[0][:, :, :], ew1[ei + 1].rearrange("(c p) h -> p c h", p=128), "dm")
            p.dma("gpsimd", osl[1][:, :, :], ew2[ei + 1].rearrange("(c p) h -> p c h", p=128), "dm")
        ld_this = ld_prev
        ld_prev = p.val("dm")
        emit_mlp(nc, p, t, w1=sl[0], w2=sl[1], e_idx=ei, wait_dm=ld_this, wait_ind=ld_ind15)
        slot_free_mm[ei % 2] = p.val("mm")


def emit_mlp(nc, p, t, w1, w2, e_idx, wait_dm, wait_ind):
    P0, P1 = t["P0"], t["P1"]
    acc, x1bf = t["acc"], t["x1bf"]
    hsc, hpl, cwbcb, cwTb = t["hsc"], t["hpl"], t["cwbcb"], t["cwTb"]
    eb1s, sb1s, sb2s, ind15b = t["eb1s"], t["sb1s"], t["sb2s"], t["ind15b"]

    ph = [P1[:, 0:512], P1[:, 512:1024]]
    pye = [P1[:, 1024:1536], P1[:, 1536:2048]]
    pcw = P0[:, 1536:2048]

    p.wait("tensor", "dm", wait_dm)
    if wait_ind:
        p.wait("tensor", "dm", wait_ind)
    for tc in range(4):
        t0 = 512 * tc
        if e_idx is not None:
            p.guard("tensor", ("pcw",))
            mmw = p.on("tensor", lambda e, e_idx=e_idx, t0=t0: e.matmul(
                pcw, ind15b[:, 128 * e_idx:128 * e_idx + 128], cwTb[0:15, t0:t0 + 512],
                start=True, stop=True), inc="mm")
            p.wait("scalar", "mm", mmw)
            p.guard("scalar", ("cwbcb",))
            sccw = p.on("scalar", lambda e: e.copy(t["cwbcb"][:, :], pcw), inc="sc")
            p.reader(("pcw",), "sc", sccw)
        for hc in range(8):
            pb = hc % 2
            p.guard("tensor", ("ph", pb))
            for cc in range(8):
                p.on("tensor", lambda e, hc=hc, cc=cc, t0=t0, pb=pb, w1=w1: e.matmul(
                    ph[pb], w1[:, cc, 128 * hc:128 * hc + 128], x1bf[:, cc, t0:t0 + 512],
                    start=(cc == 0), stop=(cc == 7)), inc="mm" if cc == 7 else None)
            mmh = p.val("mm")
            p.wait("scalar", "mm", mmh)
            if e_idx is None:
                sch = p.on("scalar", lambda e, hc=hc, pb=pb: e.activation(
                    hsc[:, hc, :], ph[pb], AF.Relu, bias=sb1s[:, hc:hc + 1]), inc="sc")
                p.reader(("ph", pb), "sc", sch)
                p.reader(("hsc", hc), "sc", sch)
            else:
                p.guard("scalar", ("hpl",))
                if hc == 0:
                    p.wait("scalar", "sc", p.val("sc"))
                sch = p.on("scalar", lambda e, hc=hc, pb=pb, e_idx=e_idx: e.activation(
                    hpl[:, :], ph[pb], AF.Relu, bias=eb1s[:, 8 * e_idx + hc:8 * e_idx + hc + 1]), inc="sc")
                p.reader(("ph", pb), "sc", sch)
                p.wait("vector", "sc", sch)
                hv = p.on("vector", lambda e, hc=hc: e.tensor_mul(hsc[:, hc, :], hpl[:, :], t["cwbcb"][:, :]), inc="ve")
                p.reader(("hpl",), "ve", hv)
                p.reader(("hsc", hc), "ve", hv)
                p.reader(("cwbcb",), "ve", hv)
        for dc in range(8):
            pb = dc % 2
            p.guard("tensor", ("pye", pb))
            if dc == 0:
                for hc in range(8):
                    key = ("hsc", hc)
                    if key in p.buf:
                        s, c = p.buf[key]
                        p.wait("tensor", s, c)
                        del p.buf[key]
            for hc in range(8):
                p.on("tensor", lambda e, dc=dc, hc=hc, pb=pb, w2=w2: e.matmul(
                    pye[pb], w2[:, hc, 128 * dc:128 * dc + 128], hsc[:, hc, :],
                    start=(hc == 0), stop=(hc == 7)), inc="mm" if hc == 7 else None)
            mmy = p.val("mm")
            if e_idx is None:
                p.wait("scalar", "mm", mmy)
                scy = p.on("scalar", lambda e, dc=dc, pb=pb: e.add(
                    pye[pb], pye[pb], sb2s[:, dc:dc + 1]), inc="sc")
                p.wait("vector", "sc", scy)
            else:
                p.wait("vector", "mm", mmy)
            va = p.on("vector", lambda e, dc=dc, t0=t0, pb=pb: e.tensor_add(
                acc[:, dc, t0:t0 + 512], acc[:, dc, t0:t0 + 512], pye[pb]), inc="ve")
            p.reader(("pye", pb), "ve", va)



_NC_CACHE = {}


def _get_nc():
    if "nc" not in _NC_CACHE:
        _NC_CACHE["nc"] = build()
    return _NC_CACHE["nc"]


def kernel(L0_attn_x, L0_attn_w_qkv, L0_attn_w_out, L0_attn_b_out,
           L0_ffn_router_W, L0_ffn_router_b, L0_ffn_sW1, L0_ffn_sb1,
           L0_ffn_sW2, L0_ffn_sb2, L0_ffn_eW1, L0_ffn_eb1, L0_ffn_eW2,
           L0_ffn_eb2):
    x = np.asarray(L0_attn_x, dtype=np.float32)
    w_qkv = np.asarray(L0_attn_w_qkv, dtype=np.float32)
    w_out = np.asarray(L0_attn_w_out, dtype=np.float32)
    b_out = np.asarray(L0_attn_b_out, dtype=np.float32)
    router_W = np.asarray(L0_ffn_router_W, dtype=np.float32)
    router_b = np.asarray(L0_ffn_router_b, dtype=np.float32)
    sW1 = np.ascontiguousarray(np.asarray(L0_ffn_sW1, dtype=np.float32))
    sb1 = np.asarray(L0_ffn_sb1, dtype=np.float32)
    sW2 = np.ascontiguousarray(np.asarray(L0_ffn_sW2, dtype=np.float32))
    sb2 = np.asarray(L0_ffn_sb2, dtype=np.float32)
    eW1 = np.ascontiguousarray(np.asarray(L0_ffn_eW1, dtype=np.float32))
    eb1 = np.asarray(L0_ffn_eb1, dtype=np.float32)
    eW2 = np.ascontiguousarray(np.asarray(L0_ffn_eW2, dtype=np.float32))
    eb2 = np.ascontiguousarray(np.asarray(L0_ffn_eb2, dtype=np.float32))

    wqkvT = np.ascontiguousarray(w_qkv.T)
    wq = np.ascontiguousarray(wqkvT[:, 0:D])
    wk = np.ascontiguousarray(wqkvT[:, D:2 * D])
    wv = np.ascontiguousarray(wqkvT[:, 2 * D:3 * D])
    wo = np.ascontiguousarray(w_out.T)
    bo = np.ascontiguousarray(b_out.reshape(8, 128).T)
    sb1m = np.ascontiguousarray(sb1.reshape(8, 128).T)
    sb2m = np.ascontiguousarray(sb2.reshape(8, 128).T)
    eb1m = np.ascontiguousarray(
        eb1.reshape(E, 8, 128).transpose(2, 0, 1).reshape(128, E * 8))
    rbm = np.ascontiguousarray(router_b.reshape(E, 1))
    identm = np.eye(128, dtype=np.float32)
    ind2 = np.zeros((2, 256), dtype=np.float32)
    ind2[0, 0:128] = 1.0
    ind2[1, 128:256] = 1.0
    ind15 = np.zeros((E, E * 128), dtype=np.float32)
    for e in range(E):
        ind15[e, 128 * e:128 * e + 128] = 1.0

    common = dict(wq=wq, wk=wk, wv=wv, wo=wo, bo=bo, rw=router_W, rb=rbm,
                  sw1=sW1, sb1=sb1m, sw2=sW2, sb2=sb2m, ew1=eW1, eb1=eb1m,
                  ew2=eW2, eb2=eb2, ident_in=identm, ind2_in=ind2,
                  ind15_in=ind15)

    in_maps = []
    for c in range(8):
        b = c // 2
        xTb = x[b].T
        if c % 2 == 0:
            xperm = np.ascontiguousarray(xTb)
        else:
            xperm = np.ascontiguousarray(
                np.concatenate([xTb[:, T:], xTb[:, :T]], axis=1))
        m = dict(common)
        m["xt"] = xperm
        in_maps.append(m)

    nc = _get_nc()
    res = run_bass_kernel_spmd(nc, in_maps, core_ids=list(range(8)))

    out = np.empty((B, NTOK, D), dtype=np.float32)
    for c in range(8):
        b = c // 2
        sl = slice(0, T) if c % 2 == 0 else slice(T, NTOK)
        out[b, sl, :] = res.results[c]["yt"].T
    return out


# version 12
# speedup vs baseline: 1.1222x; 1.1222x over previous
import numpy as np
import ml_dtypes
import concourse.bass as bass
import concourse.mybir as mybir
from concourse.bass_utils import run_bass_kernel_spmd

F32 = mybir.dt.float32
BF16 = mybir.dt.bfloat16
AF = mybir.ActivationFunctionType
ALU = mybir.AluOpType
AX = mybir.AxisListType

B, NTOK, D, NH, DH, E = 4, 4096, 1024, 16, 64, 15
T = 2048
NPAIR = 8
SC_QK = float(0.125 ** 0.5)
NEGB = 1.0e5
EPS = 1e-5

ENGS = ("tensor", "scalar", "vector", "gpsimd", "sync")


class Prog:

    def __init__(self):
        self.q = {e: [] for e in ENGS}
        self.sems = {}
        self.cnt = {}
        self.buf = {}
        self.strict = False

    def add_sem(self, name, handle):
        self.sems[name] = handle
        self.cnt[name] = 0

    def on(self, eng, fn, inc=None, n=1):
        if eng == "vector" and self.strict:
            self.wait("vector", "ve", self.cnt["ve"])
            if inc is None:
                inc = "ve"
        if inc is None:
            self.q[eng].append(fn)
            return None
        self.cnt[inc] += n
        s = self.sems[inc]
        self.q[eng].append(lambda e, fn=fn, s=s, n=n: fn(e).then_inc(s, n))
        return self.cnt[inc]

    def dma(self, eng, out, in_, sem="dm", **kw):
        self.cnt[sem] += 16
        s = self.sems[sem]
        self.q[eng].append(
            lambda e, out=out, in_=in_, s=s, kw=kw:
                e.dma_start(out=out, in_=in_, **kw).then_inc(s, 16))
        return self.cnt[sem]

    def wait(self, eng, sem, val):
        if val is None or val <= 0:
            return
        s = self.sems[sem]
        self.q[eng].append(lambda e, s=s, v=val: e.wait_ge(s, v))

    def val(self, sem):
        return self.cnt[sem]

    def reader(self, key, sem, count):
        self.buf[key] = (sem, count)

    def guard(self, eng, key):
        if key in self.buf:
            s, c = self.buf[key]
            self.wait(eng, s, c)
            del self.buf[key]


def build():
    nc = bass.Bass("TRN2", target_bir_lowering=False,
                   detect_race_conditions=False)

    io = {}
    for nm, shp in (("xt", [D, NTOK]), ("wq", [D, D]), ("wk", [D, D]),
                    ("wv", [D, D]), ("wo", [D, D]), ("bo", [128, 8]),
                    ("rw", [D, E]), ("rb", [E, 1]), ("sw1", [D, D]),
                    ("sb1", [128, 8]), ("sw2", [D, D]), ("sb2", [128, 8]),
                    ("ew1", [E, D, D]), ("eb1", [128, E * 8]),
                    ("ew2", [E, D, D]), ("eb2", [E, D]),
                    ("ident_in", [128, 128]), ("ind2_in", [2, 256]),
                    ("ind15_in", [E, E * 128])):
        io[nm] = nc.dram_tensor(nm, shp, F32, kind="ExternalInput").ap()
    io["yt"] = nc.dram_tensor("yt", [D, T], F32, kind="ExternalOutput").ap()

    base = (nc.sbuf_base + 31) // 32 * 32

    def A(nm, shp, dt, offset):
        return nc.alloc_sbuf_tensor_at(nm, shp, dt, offset=base + offset)

    t = dict(io)
    t["acc"] = A("acc", [128, 8, T], F32, offset=0)
    t["inner"] = A("inner", [128, 8, T], BF16, offset=65536)
    t["x1bf"] = A("x1bf", [128, 8, T], BF16, offset=65536)
    t["S0"] = A("S0", [128, 8, D], BF16, offset=98304)
    t["S1"] = A("S1", [128, 8, D], BF16, offset=114688)

    PA = 131072
    t["qkT"] = A("qkT", [128, 2, NTOK], BF16, offset=PA)
    t["v3d"] = A("v3d", [128, 32, 130], BF16, offset=PA + 16384)
    t["scrbf"] = A("scrbf", [128, 2048], BF16, offset=PA + 24704)
    t["bandsb"] = A("bandsb", [64, 1024], BF16, offset=PA + 26752)
    t["expsc"] = A("expsc", [128, 2, 1024], BF16, offset=PA + 28800)
    t["innerraw"] = A("innerraw", [64, T], BF16, offset=PA + 32896)
    t["fbf"] = A("fbf", [65, T], BF16, offset=PA + 36992)
    t["xstr"] = A("xstr", [128, 2, 8, 256], BF16, offset=PA + 41088)
    t["wqp"] = A("wqp", [128, 2, 8, 128], BF16, offset=PA + 49280)
    t["wkp"] = A("wkp", [128, 2, 8, 128], BF16, offset=PA + 53376)
    t["wvp"] = A("wvp", [128, 2, 8, 128], BF16, offset=PA + 57472)
    SM = PA + 61568
    for nm, shp, dt, off in (
            ("Sall", [128, 2, 32], F32, 0), ("dgdot", [128, 2, 32], F32, 256),
            ("dgexp", [128, 2, 32], F32, 512), ("keep", [128, 2, 32], F32, 768),
            ("biask", [128, 2, 32], F32, 1024), ("S2p", [128, 8], F32, 1280),
            ("sums2", [128, 4], F32, 1312), ("stbc", [128, 4], F32, 1344),
            ("negM", [128, 2], F32, 1376), ("nm15", [128, 2], F32, 1408),
            ("nmq", [2, 8], F32, 1440), ("nmk", [2, 8], F32, 1472),
            ("nqx", [2, 2], F32, 1504), ("Mv", [2, 1], F32, 1536),
            ("Mvb", [2, 1], BF16, 1568), ("statf", [1, 4], F32, 1600),
            ("tscr", [1, 4], F32, 1632), ("kqsb", [16, 128], F32, 1664),
            ("dxt", [128, 128], F32, 2176), ("ind2b", [2, 256], BF16, 2688)):
        t[nm] = A(nm, shp, dt, offset=SM + off)

    t["S2s"] = A("S2s", [128, 8, D], BF16, offset=PA)
    t["S3s"] = A("S3s", [128, 8, D], BF16, offset=PA + 16384)
    PC = PA + 32768
    for nm, shp, dt, off in (
            ("ident_c", [128, 128], F32, 0), ("ones1f_c", [1, 128], F32, 512),
            ("onescb", [128, 1], BF16, 1024), ("bos", [128, 8], F32, 1056),
            ("sb1s", [128, 8], F32, 1088), ("sb2s", [128, 8], F32, 1120),
            ("lnsc", [1, 512], F32, 2048),
            ("ybfpre", [128, 8, 512], BF16, 4096),
            ("hsc", [128, 8, 512], BF16, 12288), ("cwT", [E, T], F32, 20480),
            ("cwTb", [E, T], BF16, 28672), ("cwbcb", [128, 512], BF16, 32768),
            ("hpl", [128, 512], BF16, 33792), ("ltok", [128, 4, E], F32, 34816),
            ("ltk2", [128, 4, E], F32, 35072), ("ltk3", [128, 4, E], F32, 35328),
            ("ltm", [128, 4, 6], F32, 35584), ("eb2s", [E, D], F32, 35712),
            ("lnsm", [1, 1024], F32, 39808), ("ind15b", [E, E * 128], BF16, 43904),
            ("rwb", [128, 8, E], BF16, 47744), ("rbs", [E, 1], F32, 47984 + 16),
            ("eb1s", [128, E * 8], F32, 48032)):
        t[nm] = A(nm, shp, dt, offset=PC + off)

    PK = SM + 3200
    t["ident"] = A("ident", [128, 128], F32, offset=PK)
    t["ind65"] = A("ind65", [65, 64], BF16, offset=PK + 512)
    t["ones1f"] = A("ones1f", [1, 128], F32, offset=PK + 640)
    t["ones128f"] = A("ones128f", [128, 1], F32, offset=PK + 1152)
    t["ones2b"] = A("ones2b", [128, 2], BF16, offset=PK + 1184)

    t["P0"] = nc.alloc_psum_tensor("P0", [128, 2048], F32)
    t["P1"] = nc.alloc_psum_tensor("P1", [128, 2048], F32)

    p = Prog()
    with (
        nc.semaphore("dm") as _dm,
        nc.semaphore("sd") as _sd,
        nc.semaphore("mm") as _mm,
        nc.semaphore("sc") as _sc,
        nc.semaphore("ve") as _ve,
        nc.Block() as block,
    ):
        for nm, h in (("dm", _dm), ("sd", _sd), ("mm", _mm), ("sc", _sc), ("ve", _ve)):
            p.add_sem(nm, h)

        emit(nc, p, t)

        block.sync(lambda e: [f(e) for f in p.q["sync"]])
        block.gpsimd(lambda e: [f(e) for f in p.q["gpsimd"]])
        block.tensor(lambda e: [f(e) for f in p.q["tensor"]])
        block.scalar(lambda e: [f(e) for f in p.q["scalar"]])
        block.vector(lambda e: [f(e) for f in p.q["vector"]])
    return nc



def emit(nc, p, t):
    P0, P1 = t["P0"], t["P1"]
    acc, inner = t["acc"], t["inner"]

    p.dma("sync", t["ident"][:, :], t["ident_in"], "sd")
    for cc in range(8):
        p.dma("sync", acc[:, cc, :], t["xt"][128 * cc:128 * cc + 128, 0:T], "sd")
    ld_xres = p.val("sd")

    p.dma("gpsimd", t["ind2b"][:, :], t["ind2_in"], "dm")
    ld_ind2 = p.val("dm")

    v = t
    p.on("vector", lambda e: e.memset(v["ones1f"][:, :], 1.0))
    p.on("vector", lambda e: e.memset(v["ones128f"][:, :], 1.0))
    p.on("vector", lambda e: e.memset(v["onescb"][:, :], 1.0))
    p.on("vector", lambda e: e.memset(v["ones2b"][0:64, 0:1], 1.0))
    p.on("vector", lambda e: e.memset(v["ones2b"][64:128, 0:1], 0.0))
    p.on("vector", lambda e: e.memset(v["ones2b"][0:64, 1:2], 0.0))
    p.on("vector", lambda e: e.memset(v["ones2b"][64:128, 1:2], 1.0))
    p.on("vector", lambda e: e.memset(v["ind65"][0:64, :], 0.0))
    p.on("vector", lambda e: e.memset(v["ind65"][64:65, :], 1.0))
    p.on("vector", lambda e: e.memset(v["fbf"][0:64, :], 0.0))
    p.on("vector", lambda e: e.memset(v["v3d"][:, :, 64:65], 1.0))
    ve_const = p.on("vector", lambda e: e.memset(v["v3d"][:, :, 129:130], 1.0), inc="ve")

    st = dict(ld_ind2=ld_ind2, ve_const=ve_const)

    p.dma("sync", t["eb1s"][:, :], t["eb1"], "sd")
    p.dma("sync", t["eb2s"][:, :], t["eb2"], "sd")
    p.dma("sync", t["rbs"][:, :], t["rb"], "sd")
    ld_late = p.val("sd")

    ld_wo = ld_rw = ld_ind15 = None
    for pair in range(NPAIR):
        emit_pair(nc, p, t, pair, st)
        if pair == 0:
            for cc in range(8):
                p.dma("gpsimd", t["S0"][:, cc, :], t["wo"][128 * cc:128 * cc + 128, :], "dm")
            ld_wo = p.val("dm")
            p.dma("gpsimd", t["rwb"][:, :, :], t["rw"].rearrange("(c p) e -> p c e", p=128), "dm")
            ld_rw = p.val("dm")
            p.dma("gpsimd", t["ind15b"][:, :], t["ind15_in"], "dm")
            ld_ind15 = p.val("dm")

    attn_mm = p.val("mm")
    attn_sc = p.val("sc")
    attn_ve = p.val("ve")
    attn_dm = p.val("dm")

    p.wait("sync", "ve", attn_ve)
    p.wait("sync", "sc", attn_sc)
    p.wait("sync", "mm", attn_mm)
    p.wait("sync", "dm", attn_dm)
    p.dma("sync", t["ident_c"][:, :], t["ident_in"], "sd")
    p.dma("sync", t["bos"][:, :], t["bo"], "sd")
    p.dma("sync", t["sb1s"][:, :], t["sb1"], "sd")
    p.dma("sync", t["sb2s"][:, :], t["sb2"], "sd")
    ld_consts = p.val("sd")
    p.wait("vector", "dm", attn_dm)
    p.on("vector", lambda e: e.memset(t["ones1f_c"][:, :], 1.0))
    p.on("vector", lambda e: e.memset(t["onescb"][:, :], 1.0), inc="ve")

    p.wait("gpsimd", "ve", attn_ve)
    p.wait("gpsimd", "sc", attn_sc)
    p.wait("gpsimd", "mm", attn_mm)
    for cc in range(8):
        p.dma("gpsimd", t["S2s"][:, cc, :], t["sw1"][128 * cc:128 * cc + 128, :], "dm")
    for cc in range(8):
        p.dma("gpsimd", t["S3s"][:, cc, :], t["sw2"][128 * cc:128 * cc + 128, :], "dm")
    ld_sw = p.val("dm")

    emit_outproj(nc, p, t, ld_xres, ld_wo, ld_consts, attn_ve)
    emit_ln(nc, p, t, out_bf=True)
    b_mm = p.val("mm")

    p.wait("gpsimd", "mm", b_mm)
    p.dma("gpsimd", t["S0"][:, :, :], t["ew1"][0].rearrange("(c p) h -> p c h", p=128), "dm")
    p.dma("gpsimd", t["S1"][:, :, :], t["ew2"][0].rearrange("(c p) h -> p c h", p=128), "dm")
    ld_e0 = p.val("dm")

    emit_ffn(nc, p, t, ld_sw, ld_e0, ld_late, ld_rw, ld_ind15)

    emit_ln(nc, p, t, out_bf=False)
    d_ve = p.val("ve")
    p.wait("sync", "ve", d_ve)
    p.dma("sync", t["yt"].rearrange("(c p) t -> p c t", p=128), acc[:, :, :], "sd")
    p.wait("sync", "sd", p.val("sd"))


def emit_pair(nc, p, t, pair, st):
    P0, P1 = t["P0"], t["P1"]
    qkT, v3d, scrbf, expsc = t["qkT"], t["v3d"], t["scrbf"], t["expsc"]
    xstr, wqp, wkp, wvp = t["xstr"], t["wqp"], t["wkp"], t["wvp"]
    bandsb, innerraw, fbf, inner = t["bandsb"], t["innerraw"], t["fbf"], t["inner"]
    ident, ind65, ind2b = t["ident"], t["ind65"], t["ind2b"]
    ones1f, ones128f, ones2b = t["ones1f"], t["ones128f"], t["ones2b"]
    Sall, dgdot, dgexp, keepv, biask = t["Sall"], t["dgdot"], t["dgexp"], t["keep"], t["biask"]
    S2p, sums2, stbc, negM, nm15 = t["S2p"], t["sums2"], t["stbc"], t["negM"], t["nm15"]
    nmq, nmk, nqx, Mv, Mvb, statf, tscr = t["nmq"], t["nmk"], t["nqx"], t["Mv"], t["Mvb"], t["statf"], t["tscr"]
    kqsb, dxt = t["kqsb"], t["dxt"]
    xt, wq, wk, wv = t["xt"], t["wq"], t["wk"], t["wv"]

    col0 = 128 * pair
    wb = pair % 2

    p.guard("gpsimd", ("wqp", wb))
    for cc in range(8):
        p.dma("gpsimd", wqp[:, wb, cc, :], wq[128 * cc:128 * cc + 128, col0:col0 + 128], "dm")
        p.dma("gpsimd", wkp[:, wb, cc, :], wk[128 * cc:128 * cc + 128, col0:col0 + 128], "dm")
        p.dma("gpsimd", wvp[:, wb, cc, :], wv[128 * cc:128 * cc + 128, col0:col0 + 128], "dm")
    ld_w = p.val("dm")

    pq = [P0[:, 0:256], P0[:, 512:768]]
    pk = [P0[:, 1024:1280], P0[:, 1536:1792]]
    pv = [P1[:, 0:128], P1[:, 512:640]]
    pnq, pnk = P1[0:2, 1024:1536], P1[0:2, 1536:2048]

    for tc in range(16):
        xb = tc % 2
        p.guard("gpsimd", ("xstr", xb))
        for cc in range(8):
            p.dma("gpsimd", xstr[:, xb, cc, :],
                  xt[128 * cc:128 * cc + 128, 256 * tc:256 * tc + 256], "dm")
        xld = p.val("dm")

        pb = tc % 2
        p.wait("tensor", "dm", max(ld_w, xld))
        p.guard("tensor", ("pq", pb))
        for cc in range(8):
            p.on("tensor", lambda e, cc=cc, pb=pb, wb=wb, xb=xb: e.matmul(
                pq[pb], wqp[:, wb, cc, :], xstr[:, xb, cc, :],
                start=(cc == 0), stop=(cc == 7)), inc="mm" if cc == 7 else None)
        mmq = p.val("mm")
        p.guard("tensor", ("pk", pb))
        for cc in range(8):
            p.on("tensor", lambda e, cc=cc, pb=pb, wb=wb, xb=xb: e.matmul(
                pk[pb], wkp[:, wb, cc, :], xstr[:, xb, cc, :],
                start=(cc == 0), stop=(cc == 7)), inc="mm" if cc == 7 else None)
        mmk = p.val("mm")
        for tt4 in range(2):
            tt = 2 * tc + tt4
            vb = tt % 2
            p.guard("tensor", ("pv", vb))
            if tt == 0:
                p.wait("tensor", "ve", st["ve_const"])
            for cc in range(8):
                p.on("tensor", lambda e, cc=cc, vb=vb, wb=wb, xb=xb, tt4=tt4: e.matmul(
                    pv[vb], xstr[:, xb, cc, 128 * tt4:128 * tt4 + 128], wvp[:, wb, cc, :],
                    start=(cc == 0), stop=(cc == 7)), inc="mm" if cc == 7 else None)
            mmv = p.val("mm")
            p.wait("vector", "mm", mmv)
            p.on("vector", lambda e, tt=tt, vb=vb: e.tensor_copy(v3d[:, tt, 0:64], pv[vb][:, 0:64]))
            vv = p.on("vector", lambda e, tt=tt, vb=vb: e.tensor_copy(v3d[:, tt, 65:129], pv[vb][:, 64:128]), inc="ve")
            p.reader(("pv", vb), "ve", vv)
        p.reader(("xstr", xb), "mm", p.val("mm"))

        p.wait("vector", "mm", mmq)
        sq = p.on("vector", lambda e, tc=tc, pb=pb: e.tensor_scalar_mul(
            qkT[:, 0, 256 * tc:256 * tc + 256], pq[pb], SC_QK), inc="ve")
        p.reader(("pq", pb), "ve", sq)
        p.wait("vector", "mm", mmk)
        sk = p.on("vector", lambda e, tc=tc, pb=pb: e.tensor_scalar_mul(
            qkT[:, 1, 256 * tc:256 * tc + 256], pk[pb], SC_QK), inc="ve")
        p.reader(("pk", pb), "ve", sk)
    proj_sc = p.val("sc")
    p.reader(("wqp", wb), "mm", p.val("mm"))
    p.reader(("xstr", 0), "mm", p.val("mm"))
    p.reader(("xstr", 1), "mm", p.val("mm"))

    p.wait("vector", "ve", p.val("ve"))
    last_red = None
    for si, (pn, nmx) in enumerate(((pnq, nmq), (pnk, nmk))):
        for half in range(2):
            p.wait("vector", "sc", proj_sc)
            p.guard("vector", ("scrbf",))
            sqv = p.on("vector", lambda e, si=si, half=half: e.tensor_mul(
                scrbf[:, :], qkT[:, si, 2048 * half:2048 * half + 2048],
                qkT[:, si, 2048 * half:2048 * half + 2048]), inc="ve")
            for c4 in range(4):
                p.wait("tensor", "ve", sqv)
                if last_red:
                    p.wait("tensor", "ve", last_red)
                mmn = p.on("tensor", lambda e, c4=c4, pn=pn: e.matmul(
                    pn, ones2b[:, :], scrbf[:, 512 * c4:512 * c4 + 512],
                    start=True, stop=True), inc="mm")
                p.wait("vector", "mm", mmn)
                last_red = p.on("vector", lambda e, half=half, c4=c4, pn=pn, nmx=nmx: e.reduce_max(
                    nmx[:, 4 * half + c4:4 * half + c4 + 1], pn, axis=AX.X), inc="ve")
            p.reader(("scrbf",), "mm", p.val("mm"))
    p.strict = True
    p.on("vector", lambda e: e.reduce_max(nqx[:, 0:1], nmq[:, :], axis=AX.X))
    p.on("vector", lambda e: e.reduce_max(nqx[:, 1:2], nmk[:, :], axis=AX.X))
    p.on("vector", lambda e: e.tensor_add(Mv[:, :], nqx[:, 0:1], nqx[:, 1:2]))
    p.on("vector", lambda e: e.tensor_scalar_mul(Mv[:, :], Mv[:, :], 0.5))
    mvb_ve = p.on("vector", lambda e: e.tensor_copy(Mvb[:, :], Mv[:, :]), inc="ve")

    pMbc = P1[:, 1024:1026]
    p.wait("tensor", "ve", mvb_ve)
    p.wait("tensor", "dm", st["ld_ind2"])
    for h in range(2):
        p.on("tensor", lambda e, h=h: e.matmul(
            pMbc[:, h:h + 1], ind2b[:, 128 * h:128 * h + 128], Mvb[:, :],
            start=True, stop=True), inc="mm" if h == 1 else None)
    mmbc = p.val("mm")
    p.wait("vector", "mm", mmbc)
    p.on("vector", lambda e: e.tensor_scalar_mul(negM[:, :], pMbc[:, :], -1.0))
    negm_ve = p.on("vector", lambda e: e.tensor_scalar_add(nm15[:, :], negM[:, :], -NEGB), inc="ve")
    p.strict = False

    pP = [P0[:, 0:2048], P1[:, 0:2048]]
    prev_exp = [None, None]
    prev_dgp = [None, None]
    sve_hist = {}
    first = True
    for qt in range(32):
        gstar = qt // 16
        for step in range(4):
            h = step % 2
            grp = step // 2
            pp = pP[h]
            hs = 64 * h
            if first:
                p.wait("tensor", "sc", proj_sc)
                p.wait("tensor", "ve", negm_ve)
                first = False
            if prev_exp[h]:
                p.wait("tensor", "sc", prev_exp[h])
            if prev_dgp[h]:
                p.wait("tensor", "ve", prev_dgp[h])
                prev_dgp[h] = None
            for kc in range(4):
                p.on("tensor", lambda e, qt=qt, grp=grp, kc=kc, pp=pp, hs=hs: e.matmul(
                    pp[:, 512 * kc:512 * kc + 512],
                    qkT[hs:hs + 64, 0, 128 * qt:128 * qt + 128],
                    qkT[hs:hs + 64, 1, 2048 * grp + 512 * kc:2048 * grp + 512 * kc + 512],
                    start=True, stop=True, tile_position=(hs, 0)),
                    inc="mm" if kc == 3 else None)
            mmp = p.val("mm")
            p.wait("scalar", "mm", mmp)
            if qt == 0 and step == 0:
                p.wait("scalar", "ve", negm_ve)
            if qt >= 2 and step == 0:
                p.wait("scalar", "ve", sve_hist[qt - 2])
            p.guard("scalar", ("scrbf",))
            slot = 4 * h + 2 * (qt % 2) + grp
            prev_exp[h] = p.on("scalar", lambda e, pp=pp, h=h, slot=slot: e.activation(
                scrbf[:, 0:2048], pp, AF.Exp, bias=negM[:, h:h + 1],
                accum_out=S2p[:, slot:slot + 1]), inc="sc")
            if grp == gstar:
                off = (128 * qt) % 2048
                p.wait("vector", "sc", prev_exp[h])
                p.on("vector", lambda e, off=off, pp=pp: e.tensor_mul(dxt[:, :], pp[:, off:off + 128], ident[:, :]))
                prev_dgp[h] = p.on("vector", lambda e, qt=qt, h=h: e.reduce_sum(
                    dgdot[:, h, qt:qt + 1], dxt[:, :], axis=AX.X), inc="ve")
        p.wait("vector", "sc", p.val("sc"))
        b0 = 2 * (qt % 2)
        p.on("vector", lambda e, qt=qt, b0=b0: e.reduce_sum(Sall[:, 0, qt:qt + 1], S2p[:, b0:b0 + 2], axis=AX.X))
        sve_hist[qt] = p.on("vector", lambda e, qt=qt, b0=b0: e.reduce_sum(Sall[:, 1, qt:qt + 1], S2p[:, 4 + b0:4 + b0 + 2], axis=AX.X), inc="ve")
    pass1_sc = p.val("sc")

    p.wait("scalar", "ve", p.val("ve"))
    sc_dg = None
    for h in range(2):
        sc_dg = p.on("scalar", lambda e, h=h: e.activation(
            dgexp[:, h, :], dgdot[:, h, :], AF.Exp, bias=negM[:, h:h + 1]), inc="sc")
    p.wait("vector", "sc", sc_dg)
    p.strict = True
    p.on("vector", lambda e: e.reciprocal(
        dxt[:, 0:64], Sall[:, :, :].rearrange("p a b -> p (a b)")))
    p.on("vector", lambda e: e.tensor_mul(
        dgexp[:, :, :].rearrange("p a b -> p (a b)"),
        dgexp[:, :, :].rearrange("p a b -> p (a b)"), dxt[:, 0:64]))
    for h in range(2):
        p.on("vector", lambda e, h=h: e.reduce_sum(sums2[:, 2 * h:2 * h + 1], dgexp[:, h, :], axis=AX.X))
        p.on("vector", lambda e, h=h: e.tensor_mul(dxt[:, 0:32], dgexp[:, h, :], dgexp[:, h, :]))
        p.on("vector", lambda e, h=h: e.reduce_sum(sums2[:, 2 * h + 1:2 * h + 2], dxt[:, 0:32], axis=AX.X), inc="ve")
    st_ve = p.val("ve")
    pst = P0[0:1, 0:4]
    p.wait("tensor", "ve", st_ve)
    mmst = p.on("tensor", lambda e: e.matmul(pst, ones128f[:, :], sums2[:, :], start=True, stop=True), inc="mm")
    p.wait("vector", "mm", mmst)
    n = float(NTOK)
    p.on("vector", lambda e: e.tensor_copy(tscr[:, :], pst))
    for h in range(2):
        p.on("vector", lambda e, h=h: e.tensor_scalar_mul(statf[:, 2 * h:2 * h + 1], tscr[:, 2 * h:2 * h + 1], 1.0 / n))
        p.on("vector", lambda e, h=h: e.tensor_mul(statf[:, 2 * h + 1:2 * h + 2], tscr[:, 2 * h:2 * h + 1], tscr[:, 2 * h:2 * h + 1]))
        p.on("vector", lambda e, h=h: e.tensor_scalar_mul(statf[:, 2 * h + 1:2 * h + 2], statf[:, 2 * h + 1:2 * h + 2], 1.0 / n))
        p.on("vector", lambda e, h=h: e.tensor_sub(statf[:, 2 * h + 1:2 * h + 2], tscr[:, 2 * h + 1:2 * h + 2], statf[:, 2 * h + 1:2 * h + 2]))
        p.on("vector", lambda e, h=h: e.tensor_scalar_mul(statf[:, 2 * h + 1:2 * h + 2], statf[:, 2 * h + 1:2 * h + 2], 2.25 / (n - 1.0)), inc="ve")
    stf_ve = p.val("ve")
    pbc = P0[:, 512:516]
    p.wait("tensor", "ve", stf_ve)
    mmbc2 = p.on("tensor", lambda e: e.matmul(pbc, ones1f[:, :], statf[:, :], start=True, stop=True), inc="mm")
    p.wait("vector", "mm", mmbc2)
    p.on("vector", lambda e: e.tensor_copy(stbc[:, :], pbc), inc="ve")
    for h in range(2):
        p.on("vector", lambda e, h=h: e.tensor_scalar(dxt[:, 0:32], dgexp[:, h, :], stbc[:, 2 * h:2 * h + 1], None, ALU.subtract))
        p.on("vector", lambda e: e.tensor_mul(dxt[:, 0:32], dxt[:, 0:32], dxt[:, 0:32]))
        p.on("vector", lambda e, h=h: e.tensor_scalar(keepv[:, h, :], dxt[:, 0:32], stbc[:, 2 * h + 1:2 * h + 2], None, ALU.is_le))
        p.on("vector", lambda e, h=h: e.tensor_scalar(biask[:, h, :], keepv[:, h, :], NEGB, nm15[:, h:h + 1], ALU.mult, ALU.add), inc="ve")
    keep_ve = p.val("ve")
    p.strict = False

    ping = [P0[:, 0:1024], P0[:, 1024:2048]]
    pav = P1[0:65, 0:1024]
    pband = P1[0:64, 1024:2048]
    prow = P1[0:16, 1024:1152]
    keeprow = fbf[64:65, :]
    bandsrc = fbf[0:65, :]
    ev_exp = [None, None]
    prev_norm_ve = None
    for h in range(2):
        hs = 64 * h
        p.wait("tensor", "ve", keep_ve if h == 0 else prev_norm_ve)
        mkq = p.on("tensor", lambda e, h=h: e.transpose(prow, keepv[:, h, 0:16], ident[:, :]), inc="mm")
        p.wait("vector", "mm", mkq)
        p.guard("vector", ("kqsb",))
        vkq = p.on("vector", lambda e: e.tensor_copy(kqsb[:, :], prow), inc="ve")
        p.wait("gpsimd", "ve", vkq)
        p.guard("gpsimd", ("fbfrow",))
        kqd = p.dma("gpsimd", fbf[64:65, 0:2048], kqsb[:, :], "dm")
        p.reader(("kqsb",), "dm", kqd)
        kq_done = p.val("dm")

        for qh in range(2):
            q0 = 1024 * qh
            for kt in range(33):
                if kt < 32:
                    pb = kt % 2
                    if ev_exp[pb]:
                        p.wait("tensor", "sc", ev_exp[pb])
                    if h == 0 and qh == 0 and kt == 0:
                        p.wait("tensor", "ve", p.val("ve"))
                    for qc in range(2):
                        p.on("tensor", lambda e, hs=hs, kt=kt, q0=q0, qc=qc, pb=pb: e.matmul(
                            ping[pb][:, 512 * qc:512 * qc + 512],
                            qkT[hs:hs + 64, 1, 128 * kt:128 * kt + 128],
                            qkT[hs:hs + 64, 0, q0 + 512 * qc:q0 + 512 * qc + 512],
                            start=True, stop=True, tile_position=(hs, 0)),
                            inc="mm" if qc == 1 else None)
                    mm2 = p.val("mm")
                    p.wait("scalar", "mm", mm2)
                    if h == 0 and qh == 0 and kt == 0:
                        p.wait("scalar", "ve", keep_ve)
                    p.guard("scalar", ("expsc", pb))
                    ev_exp[pb] = p.on("scalar", lambda e, h=h, kt=kt, pb=pb: e.activation(
                        expsc[:, pb, :], ping[pb], AF.Exp, bias=biask[:, h, kt:kt + 1]), inc="sc")
                if kt >= 1:
                    akt = kt - 1
                    apb = akt % 2
                    p.wait("tensor", "sc", ev_exp[apb])
                    if akt == 0 and prev_norm_ve:
                        p.wait("tensor", "ve", prev_norm_ve)
                    for qc in range(2):
                        p.on("tensor", lambda e, h=h, akt=akt, qc=qc, apb=apb: e.matmul(
                            pav[:, 512 * qc:512 * qc + 512],
                            v3d[:, akt, 65 * h:65 * h + 65],
                            expsc[:, apb, 512 * qc:512 * qc + 512],
                            start=(akt == 0), stop=(akt == 31)),
                            inc="mm" if qc == 1 else None)
                    p.reader(("expsc", apb), "mm", p.val("mm"))
            mm_av = p.val("mm")
            p.wait("vector", "mm", mm_av)
            if qh == 0:
                p.wait("vector", "dm", kq_done)
            p.on("vector", lambda e: e.reciprocal(pav[64:65, :], pav[64:65, :]))
            f_ve = p.on("vector", lambda e, q0=q0: e.tensor_mul(
                keeprow[:, q0:q0 + 1024], pav[64:65, :], keeprow[:, q0:q0 + 1024]), inc="ve")
            p.wait("tensor", "ve", f_ve)
            p.on("tensor", lambda e, q0=q0: e.matmul(
                pband[:, 0:512], ind65[:, :], bandsrc[:, q0:q0 + 512],
                start=True, stop=True))
            mmband = p.on("tensor", lambda e, q0=q0: e.matmul(
                pband[:, 512:1024], ind65[:, :], bandsrc[:, q0 + 512:q0 + 1024],
                start=True, stop=True), inc="mm")
            if h == 0 and qh == 1:
                p.reader(("fbfrow",), "mm", p.val("mm"))
            p.wait("vector", "mm", mmband)
            p.on("vector", lambda e: e.tensor_copy(bandsb[:, :], pband))
            if h == 0:
                prev_norm_ve = p.on("vector", lambda e, q0=q0, pair=pair: e.tensor_mul(
                    inner[0:64, pair, q0:q0 + 1024], pav[0:64, :], bandsb[:, :]), inc="ve")
            else:
                p.guard("vector", ("innerraw", qh))
                prev_norm_ve = p.on("vector", lambda e, q0=q0: e.tensor_mul(
                    innerraw[:, q0:q0 + 1024], pav[0:64, :], bandsb[:, :]), inc="ve")
                p.wait("gpsimd", "ve", prev_norm_ve)
                dmi = p.dma("gpsimd", inner[64:128, pair, q0:q0 + 1024], innerraw[:, q0:q0 + 1024], "dm")
                p.reader(("innerraw", qh), "dm", dmi)
    p.reader(("fbfrow",), "mm", p.val("mm"))
    p.reader(("scrbf",), "mm", p.val("mm"))
    end_sc, end_ve = p.val("sc"), p.val("ve")
    for b in range(2):
        p.reader(("pq", b), "sc", end_sc)
        p.reader(("pk", b), "sc", end_sc)
        p.reader(("pv", b), "ve", end_ve)


def emit_outproj(nc, p, t, ld_xres, ld_wo, ld_consts, attn_ve):
    P0 = t["P0"]
    acc, inner, S0, bos = t["acc"], t["inner"], t["S0"], t["bos"]
    pop = [P0[:, 0:512], P0[:, 512:1024]]
    evb = [None, None]
    p.wait("tensor", "dm", ld_wo)
    p.wait("tensor", "ve", attn_ve)
    for tc in range(4):
        t0 = 512 * tc
        for dc in range(8):
            pb = (8 * tc + dc) % 2
            if evb[pb]:
                p.wait("tensor", "ve", evb[pb])
            for cc in range(8):
                p.on("tensor", lambda e, cc=cc, dc=dc, t0=t0, pb=pb: e.matmul(
                    pop[pb], S0[:, cc, 128 * dc:128 * dc + 128],
                    inner[:, cc, t0:t0 + 512],
                    start=(cc == 0), stop=(cc == 7)), inc="mm" if cc == 7 else None)
            mmo = p.val("mm")
            p.wait("scalar", "mm", mmo)
            if tc == 0 and dc == 0:
                p.wait("scalar", "sd", ld_consts)
            sco = p.on("scalar", lambda e, dc=dc, pb=pb: e.add(
                pop[pb], pop[pb], bos[:, dc:dc + 1]), inc="sc")
            p.wait("vector", "sc", sco)
            if tc == 0 and dc == 0:
                p.wait("vector", "sd", ld_xres)
            evb[pb] = p.on("vector", lambda e, dc=dc, t0=t0, pb=pb: e.tensor_add(
                acc[:, dc, t0:t0 + 512], acc[:, dc, t0:t0 + 512], pop[pb]), inc="ve")


def emit_ln(nc, p, t, out_bf):
    P1 = t["P1"]
    acc, x1bf = t["acc"], t["x1bf"]
    onescb, ones1f = t["onescb"], t["ones1f_c"]
    ybfpre, hsc, lnsm = t["ybfpre"], t["hsc"], t["lnsm"]

    ps, ps2 = P1[0:1, 0:512], P1[0:1, 512:1024]
    pmu, prs = P1[:, 1024:1536], P1[:, 1536:2048]

    p.wait("tensor", "sc", p.val("sc"))
    for tc in range(4):
        t0 = 512 * tc
        cast_ve = None
        for dc in range(8):
            p.on("vector", lambda e, dc=dc, t0=t0: e.tensor_copy(ybfpre[:, dc, :], acc[:, dc, t0:t0 + 512]))
            cast_ve = p.on("vector", lambda e, dc=dc: e.tensor_mul(hsc[:, dc, :], ybfpre[:, dc, :], ybfpre[:, dc, :]), inc="ve")
        p.wait("tensor", "ve", cast_ve)
        for dc in range(8):
            p.on("tensor", lambda e, dc=dc: e.matmul(
                ps, onescb[:, :], ybfpre[:, dc, :], start=(dc == 0), stop=(dc == 7)),
                inc="mm" if dc == 7 else None)
        for dc in range(8):
            p.on("tensor", lambda e, dc=dc: e.matmul(
                ps2, onescb[:, :], hsc[:, dc, :], start=(dc == 0), stop=(dc == 7)),
                inc="mm" if dc == 7 else None)
        mm_s = p.val("mm")
        lnsc = t["lnsc"]
        p.wait("vector", "mm", mm_s)
        p.on("vector", lambda e: e.tensor_scalar_mul(lnsm[:, 0:512], ps, 1.0 / 1024.0))
        p.on("vector", lambda e, lnsc=lnsc: e.tensor_scalar_mul(lnsc[:, :], ps2, 1.0 / 1024.0))
        p.on("vector", lambda e: e.tensor_mul(lnsm[:, 512:1024], lnsm[0:1, 0:512], lnsm[0:1, 0:512]))
        p.on("vector", lambda e, lnsc=lnsc: e.tensor_sub(lnsc[:, :], lnsc[:, :], lnsm[0:1, 512:1024]))
        p.on("vector", lambda e, lnsc=lnsc: e.tensor_scalar_add(lnsc[:, :], lnsc[:, :], EPS))
        rec_ve = p.on("vector", lambda e, lnsc=lnsc: e.reciprocal(lnsc[:, :], lnsc[:, :]), inc="ve")
        p.wait("scalar", "ve", rec_ve)
        sq_sc = p.on("scalar", lambda e, lnsc=lnsc: e.activation(lnsm[:, 512:1024], lnsc[:, :], AF.Sqrt), inc="sc")
        p.wait("tensor", "sc", sq_sc)
        p.on("tensor", lambda e: e.matmul(pmu, ones1f[:, :], lnsm[0:1, 0:512], start=True, stop=True))
        mmb = p.on("tensor", lambda e: e.matmul(prs, ones1f[:, :], lnsm[0:1, 512:1024], start=True, stop=True), inc="mm")
        p.wait("vector", "mm", mmb)
        for dc in range(8):
            p.on("vector", lambda e, dc=dc, t0=t0: e.tensor_sub(acc[:, dc, t0:t0 + 512], acc[:, dc, t0:t0 + 512], pmu))
            lv = p.on("vector", lambda e, dc=dc, t0=t0: e.tensor_mul(acc[:, dc, t0:t0 + 512], acc[:, dc, t0:t0 + 512], prs), inc="ve")
            if out_bf:
                lv = p.on("vector", lambda e, dc=dc, t0=t0: e.tensor_copy(x1bf[:, dc, t0:t0 + 512], acc[:, dc, t0:t0 + 512]), inc="ve")


def _b3(ap):
    return ap.broadcast_to([128, 4, E])


def emit_ffn(nc, p, t, ld_sw, ld_e0, ld_late, ld_rw, ld_ind15):
    P0, P1 = t["P0"], t["P1"]
    acc, x1bf = t["acc"], t["x1bf"]
    S0, S1, S2s, S3s = t["S0"], t["S1"], t["S2s"], t["S3s"]
    rwb, rbs, cwT, cwTb = t["rwb"], t["rbs"], t["cwT"], t["cwTb"]
    ltok, ltk2, ltk3, ltm = t["ltok"], t["ltk2"], t["ltk3"], t["ltm"]
    ident, ind15b, eb2s = t["ident_c"], t["ind15b"], t["eb2s"]
    ew1, ew2 = t["ew1"], t["ew2"]

    plg = P0[0:15, 0:512]
    ptt = P0[:, 512:527]
    pct = P0[0:15, 1024:1152]
    pye = [P1[:, 1024:1536], P1[:, 1536:2048]]

    ln1_ve = p.val("ve")
    p.wait("tensor", "dm", ld_rw)
    p.wait("tensor", "ve", ln1_ve)
    p.strict = True
    for tc in range(4):
        t0 = 512 * tc
        p.guard("tensor", ("plg",))
        for cc in range(8):
            p.on("tensor", lambda e, cc=cc, t0=t0: e.matmul(
                plg, rwb[:, cc, :], x1bf[:, cc, t0:t0 + 512],
                start=(cc == 0), stop=(cc == 7)), inc="mm" if cc == 7 else None)
        mml = p.val("mm")
        p.wait("scalar", "mm", mml)
        if tc == 0:
            p.wait("scalar", "sd", ld_late)
        scl = p.on("scalar", lambda e, t0=t0: e.add(
            cwT[0:15, t0:t0 + 512], plg, rbs[:, 0:1]), inc="sc")
        p.reader(("plg",), "sc", scl)
        for j in range(4):
            p.wait("tensor", "sc", scl)
            p.guard("tensor", ("ptt",))
            mmt = p.on("tensor", lambda e, t0=t0, j=j: e.transpose(
                ptt, cwT[0:15, t0 + 128 * j:t0 + 128 * j + 128], ident[0:15, 0:15]), inc="mm")
            p.wait("vector", "mm", mmt)
            vtt = p.on("vector", lambda e, j=j: e.tensor_copy(ltok[:, j, :], ptt), inc="ve")
            p.reader(("ptt",), "ve", vtt)

        p.on("vector", lambda e: e.reduce_max(ltm[:, :, 0:1], ltok[:, :, :], axis=AX.X))
        sub_ve = p.on("vector", lambda e: e.tensor_tensor(ltk2[:, :, :], ltok[:, :, :], _b3(ltm[:, :, 0:1]), ALU.subtract), inc="ve")
        p.wait("scalar", "ve", sub_ve)
        exp_sc = p.on("scalar", lambda e: e.activation(ltk2[:, :, :], ltk2[:, :, :], AF.Exp), inc="sc")
        p.wait("vector", "sc", exp_sc)
        p.on("vector", lambda e: e.reduce_sum(ltm[:, :, 1:2], ltk2[:, :, :], axis=AX.X))
        p.on("vector", lambda e: e.reciprocal(ltm[:, :, 2:3], ltm[:, :, 1:2]))
        p.on("vector", lambda e: e.tensor_tensor(ltok[:, :, :], ltk2[:, :, :], _b3(ltm[:, :, 2:3]), ALU.mult))
        p.on("vector", lambda e: e.reduce_max(ltm[:, :, 3:4], ltok[:, :, :], axis=AX.X))
        p.on("vector", lambda e: e.tensor_tensor(ltk3[:, :, :], ltok[:, :, :], _b3(ltm[:, :, 3:4]), ALU.is_ge))
        p.on("vector", lambda e: e.scalar_tensor_tensor(ltk2[:, :, :], ltk3[:, :, :], -1e9, ltok[:, :, :], ALU.mult, ALU.add))
        p.on("vector", lambda e: e.reduce_max(ltm[:, :, 4:5], ltk2[:, :, :], axis=AX.X))
        p.on("vector", lambda e: e.tensor_tensor(ltk3[:, :, :], ltk2[:, :, :], _b3(ltm[:, :, 4:5]), ALU.is_ge))
        p.on("vector", lambda e: e.scalar_tensor_tensor(ltk2[:, :, :], ltk3[:, :, :], -1e9, ltk2[:, :, :], ALU.mult, ALU.add))
        p.on("vector", lambda e: e.reduce_max(ltm[:, :, 5:6], ltk2[:, :, :], axis=AX.X))
        p.on("vector", lambda e: e.tensor_tensor(ltk3[:, :, :], ltok[:, :, :], _b3(ltm[:, :, 5:6]), ALU.is_ge))
        cw_ve = p.on("vector", lambda e: e.tensor_mul(ltk3[:, :, :], ltk3[:, :, :], ltok[:, :, :]), inc="ve")
        for j in range(4):
            p.wait("tensor", "ve", cw_ve)
            p.guard("tensor", ("pct",))
            mmc = p.on("tensor", lambda e, j=j: e.transpose(pct, ltk3[:, j, :], ident[:, :]), inc="mm")
            p.wait("scalar", "mm", mmc)
            scc = p.on("scalar", lambda e, t0=t0, j=j: e.copy(cwT[0:15, t0 + 128 * j:t0 + 128 * j + 128], pct), inc="sc")
            p.reader(("pct",), "sc", scc)
    cw_sc = p.val("sc")
    p.wait("vector", "sc", cw_sc)
    cwb_ve = p.on("vector", lambda e: e.tensor_copy(cwTb[:, :], cwT[:, :]), inc="ve")
    p.strict = False

    p.wait("tensor", "ve", cwb_ve)
    for dc in range(8):
        for tc in range(4):
            pb = (4 * tc + dc) % 2
            p.guard("tensor", ("pye", pb))
            mme = p.on("tensor", lambda e, dc=dc, tc=tc, pb=pb: e.matmul(
                pye[pb], eb2s[:, 128 * dc:128 * dc + 128], cwT[0:15, 512 * tc:512 * tc + 512],
                start=True, stop=True), inc="mm")
            p.wait("vector", "mm", mme)
            va = p.on("vector", lambda e, dc=dc, tc=tc, pb=pb: e.tensor_add(
                acc[:, dc, 512 * tc:512 * tc + 512], acc[:, dc, 512 * tc:512 * tc + 512], pye[pb]), inc="ve")
            p.reader(("pye", pb), "ve", va)

    emit_mlp(nc, p, t, w1=S2s, w2=S3s, e_idx=None, wait_dm=ld_sw, wait_ind=None)
    slots = [(S0, S1), (S2s, S3s)]
    slot_free_mm = [None, p.val("mm")]
    ld_prev = ld_e0
    for ei in range(E):
        sl = slots[ei % 2]
        if ei + 1 < E:
            osl = slots[(ei + 1) % 2]
            p.wait("gpsimd", "mm", slot_free_mm[(ei + 1) % 2] or 0)
            p.dma("gpsimd", osl[0][:, :, :], ew1[ei + 1].rearrange("(c p) h -> p c h", p=128), "dm")
            p.dma("gpsimd", osl[1][:, :, :], ew2[ei + 1].rearrange("(c p) h -> p c h", p=128), "dm")
        ld_this = ld_prev
        ld_prev = p.val("dm")
        emit_mlp(nc, p, t, w1=sl[0], w2=sl[1], e_idx=ei, wait_dm=ld_this, wait_ind=ld_ind15)
        slot_free_mm[ei % 2] = p.val("mm")


def emit_mlp(nc, p, t, w1, w2, e_idx, wait_dm, wait_ind):
    P0, P1 = t["P0"], t["P1"]
    acc, x1bf = t["acc"], t["x1bf"]
    hsc, hpl, cwbcb, cwTb = t["hsc"], t["hpl"], t["cwbcb"], t["cwTb"]
    eb1s, sb1s, sb2s, ind15b = t["eb1s"], t["sb1s"], t["sb2s"], t["ind15b"]

    ph = [P1[:, 0:512], P1[:, 512:1024]]
    pye = [P1[:, 1024:1536], P1[:, 1536:2048]]
    pcw = P0[:, 1536:2048]

    p.wait("tensor", "dm", wait_dm)
    if wait_ind:
        p.wait("tensor", "dm", wait_ind)
    for tc in range(4):
        t0 = 512 * tc
        if e_idx is not None:
            p.guard("tensor", ("pcw",))
            mmw = p.on("tensor", lambda e, e_idx=e_idx, t0=t0: e.matmul(
                pcw, ind15b[:, 128 * e_idx:128 * e_idx + 128], cwTb[0:15, t0:t0 + 512],
                start=True, stop=True), inc="mm")
            p.wait("scalar", "mm", mmw)
            p.guard("scalar", ("cwbcb",))
            sccw = p.on("scalar", lambda e: e.copy(t["cwbcb"][:, :], pcw), inc="sc")
            p.reader(("pcw",), "sc", sccw)
        for hc in range(8):
            pb = hc % 2
            p.guard("tensor", ("ph", pb))
            for cc in range(8):
                p.on("tensor", lambda e, hc=hc, cc=cc, t0=t0, pb=pb, w1=w1: e.matmul(
                    ph[pb], w1[:, cc, 128 * hc:128 * hc + 128], x1bf[:, cc, t0:t0 + 512],
                    start=(cc == 0), stop=(cc == 7)), inc="mm" if cc == 7 else None)
            mmh = p.val("mm")
            p.wait("scalar", "mm", mmh)
            if e_idx is None:
                sch = p.on("scalar", lambda e, hc=hc, pb=pb: e.activation(
                    hsc[:, hc, :], ph[pb], AF.Relu, bias=sb1s[:, hc:hc + 1]), inc="sc")
                p.reader(("ph", pb), "sc", sch)
                p.reader(("hsc", hc), "sc", sch)
            else:
                p.guard("scalar", ("hpl",))
                if hc == 0:
                    p.wait("scalar", "sc", p.val("sc"))
                sch = p.on("scalar", lambda e, hc=hc, pb=pb, e_idx=e_idx: e.activation(
                    hpl[:, :], ph[pb], AF.Relu, bias=eb1s[:, 8 * e_idx + hc:8 * e_idx + hc + 1]), inc="sc")
                p.reader(("ph", pb), "sc", sch)
                p.wait("vector", "sc", sch)
                hv = p.on("vector", lambda e, hc=hc: e.tensor_mul(hsc[:, hc, :], hpl[:, :], t["cwbcb"][:, :]), inc="ve")
                p.reader(("hpl",), "ve", hv)
                p.reader(("hsc", hc), "ve", hv)
                p.reader(("cwbcb",), "ve", hv)
        for dc in range(8):
            pb = dc % 2
            p.guard("tensor", ("pye", pb))
            if dc == 0:
                for hc in range(8):
                    key = ("hsc", hc)
                    if key in p.buf:
                        s, c = p.buf[key]
                        p.wait("tensor", s, c)
                        del p.buf[key]
            for hc in range(8):
                p.on("tensor", lambda e, dc=dc, hc=hc, pb=pb, w2=w2: e.matmul(
                    pye[pb], w2[:, hc, 128 * dc:128 * dc + 128], hsc[:, hc, :],
                    start=(hc == 0), stop=(hc == 7)), inc="mm" if hc == 7 else None)
            mmy = p.val("mm")
            if e_idx is None:
                p.wait("scalar", "mm", mmy)
                scy = p.on("scalar", lambda e, dc=dc, pb=pb: e.add(
                    pye[pb], pye[pb], sb2s[:, dc:dc + 1]), inc="sc")
                p.wait("vector", "sc", scy)
            else:
                p.wait("vector", "mm", mmy)
            va = p.on("vector", lambda e, dc=dc, t0=t0, pb=pb: e.tensor_add(
                acc[:, dc, t0:t0 + 512], acc[:, dc, t0:t0 + 512], pye[pb]), inc="ve")
            p.reader(("pye", pb), "ve", va)



_NC_CACHE = {}


def _get_nc():
    if "nc" not in _NC_CACHE:
        _NC_CACHE["nc"] = build()
    return _NC_CACHE["nc"]


def kernel(L0_attn_x, L0_attn_w_qkv, L0_attn_w_out, L0_attn_b_out,
           L0_ffn_router_W, L0_ffn_router_b, L0_ffn_sW1, L0_ffn_sb1,
           L0_ffn_sW2, L0_ffn_sb2, L0_ffn_eW1, L0_ffn_eb1, L0_ffn_eW2,
           L0_ffn_eb2):
    x = np.asarray(L0_attn_x, dtype=np.float32)
    w_qkv = np.asarray(L0_attn_w_qkv, dtype=np.float32)
    w_out = np.asarray(L0_attn_w_out, dtype=np.float32)
    b_out = np.asarray(L0_attn_b_out, dtype=np.float32)
    router_W = np.asarray(L0_ffn_router_W, dtype=np.float32)
    router_b = np.asarray(L0_ffn_router_b, dtype=np.float32)
    sW1 = np.ascontiguousarray(np.asarray(L0_ffn_sW1, dtype=np.float32))
    sb1 = np.asarray(L0_ffn_sb1, dtype=np.float32)
    sW2 = np.ascontiguousarray(np.asarray(L0_ffn_sW2, dtype=np.float32))
    sb2 = np.asarray(L0_ffn_sb2, dtype=np.float32)
    eW1 = np.ascontiguousarray(np.asarray(L0_ffn_eW1, dtype=np.float32))
    eb1 = np.asarray(L0_ffn_eb1, dtype=np.float32)
    eW2 = np.ascontiguousarray(np.asarray(L0_ffn_eW2, dtype=np.float32))
    eb2 = np.ascontiguousarray(np.asarray(L0_ffn_eb2, dtype=np.float32))

    wqkvT = np.ascontiguousarray(w_qkv.T)
    wq = np.ascontiguousarray(wqkvT[:, 0:D])
    wk = np.ascontiguousarray(wqkvT[:, D:2 * D])
    wv = np.ascontiguousarray(wqkvT[:, 2 * D:3 * D])
    wo = np.ascontiguousarray(w_out.T)
    bo = np.ascontiguousarray(b_out.reshape(8, 128).T)
    sb1m = np.ascontiguousarray(sb1.reshape(8, 128).T)
    sb2m = np.ascontiguousarray(sb2.reshape(8, 128).T)
    eb1m = np.ascontiguousarray(
        eb1.reshape(E, 8, 128).transpose(2, 0, 1).reshape(128, E * 8))
    rbm = np.ascontiguousarray(router_b.reshape(E, 1))
    identm = np.eye(128, dtype=np.float32)
    ind2 = np.zeros((2, 256), dtype=np.float32)
    ind2[0, 0:128] = 1.0
    ind2[1, 128:256] = 1.0
    ind15 = np.zeros((E, E * 128), dtype=np.float32)
    for e in range(E):
        ind15[e, 128 * e:128 * e + 128] = 1.0

    common = dict(wq=wq, wk=wk, wv=wv, wo=wo, bo=bo, rw=router_W, rb=rbm,
                  sw1=sW1, sb1=sb1m, sw2=sW2, sb2=sb2m, ew1=eW1, eb1=eb1m,
                  ew2=eW2, eb2=eb2, ident_in=identm, ind2_in=ind2,
                  ind15_in=ind15)

    in_maps = []
    for c in range(8):
        b = c // 2
        xTb = x[b].T
        if c % 2 == 0:
            xperm = np.ascontiguousarray(xTb)
        else:
            xperm = np.ascontiguousarray(
                np.concatenate([xTb[:, T:], xTb[:, :T]], axis=1))
        m = dict(common)
        m["xt"] = xperm
        in_maps.append(m)

    nc = _get_nc()
    res = run_bass_kernel_spmd(nc, in_maps, core_ids=list(range(8)))

    out = np.empty((B, NTOK, D), dtype=np.float32)
    for c in range(8):
        b = c // 2
        sl = slice(0, T) if c % 2 == 0 else slice(T, NTOK)
        out[b, sl, :] = res.results[c]["yt"].T
    return out
